# version 16
# baseline (speedup 1.0000x reference)
import numpy as np
import ml_dtypes
import concourse.bass as bass
import concourse.mybir as mybir
from concourse.bass_utils import run_bass_kernel_spmd

F32 = mybir.dt.float32
BF16 = mybir.dt.bfloat16
AF = mybir.ActivationFunctionType
ALU = mybir.AluOpType
AX = mybir.AxisListType

DM = 1024
HEADS = 16
NPAIR = 8
SCALE = (DM // HEADS) ** -0.5
EPS = 1e-5
QTW = 512
FT = 512
SBUF_BASE = 16512
BYTES_LIMIT = 229248

ENGINES = ("tensor", "scalar", "vector", "gpsimd", "sync")


class PB:

    def __init__(self, nc, n_dma_sems=16):
        self.nc = nc
        self.n_dma_sems = n_dma_sems
        self.ops = []
        self.lw = {}
        self.rd = {}
        self.rings = {"h": [None] * n_dma_sems, "s": [None] * n_dma_sems}
        self.counts = {"h": 0, "s": 0}
        self.overlap = {}
        self.settle_emit = None

    def op(self, eng, emit, reads=(), writes=(), dma=False, forced=()):
        oid = len(self.ops)
        if self.overlap:
            ext = []
            for w in writes:
                ext.extend(self.overlap.get(w, ()))
            if ext:
                writes = tuple(writes) + tuple(ext)
        deps = set()
        for r in reads:
            w = self.lw.get(r)
            if w is not None:
                deps.add(w)
        for w in writes:
            p = self.lw.get(w)
            if p is not None:
                deps.add(p)
            deps.update(self.rd.get(w, ()))
        deps.update(forced)
        slot = pool = None
        reuse = False
        if dma:
            pool = "s" if eng == "gpsimd" else "h"
            slot = self.counts[pool] % self.n_dma_sems
            prev = self.rings[pool][slot]
            if prev is not None:
                deps.add(prev)
                deps.update(self.ops[prev].get("consumers", ()))
                reuse = True
            self.rings[pool][slot] = oid
            self.counts[pool] += 1
        deps.discard(oid)
        for r in reads:
            self.rd.setdefault(r, set()).add(oid)
            self.ops[self.lw[r]].setdefault("consumers", set()).add(oid) \
                if r in self.lw else None
        for w in writes:
            self.lw[w] = oid
            self.rd[w] = set()
        self.ops.append({"id": oid, "eng": eng, "emit": emit, "deps": deps,
                         "dma": dma, "slot": slot, "pool": pool, "inc": dma,
                         "reuse": reuse})
        return oid

    def dma(self, eng, emit, reads=(), writes=()):
        return self.op(eng, emit, reads, writes, dma=True)

    def barrier(self):
        allops = set(self.lw.values())
        for s in self.rd.values():
            allops |= s
        fr = frozenset(allops)
        for e in ENGINES:
            self.op(e, lambda eng: eng.nop(), forced=fr)

    def build(self):
        ops = self.ops
        needs = set()
        for o in ops:
            if o["eng"] != "tensor" or o["dma"]:
                continue
            for d in o["deps"]:
                dep = ops[d]
                if dep["eng"] == "vector" and not dep["dma"]:
                    needs.add(d)
        if needs and self.settle_emit is not None:
            newops = []
            remap = {}
            settle_pos = {}
            prev_sp = None
            for o in ops:
                remap[o["id"]] = len(newops)
                newops.append(o)
                if o["id"] in needs:
                    fd = {len(newops) - 1}
                    if prev_sp is not None:
                        fd.add(prev_sp)
                    newops.append({"eng": "vector", "emit": self.settle_emit,
                                   "fdeps": fd, "dma": False, "slot": None,
                                   "pool": None, "inc": False, "reuse": False})
                    settle_pos[o["id"]] = len(newops) - 1
                    prev_sp = len(newops) - 1
            for i, o in enumerate(newops):
                if "fdeps" in o:
                    o["deps"] = set(o.pop("fdeps"))
                else:
                    nd = set()
                    for d in o["deps"]:
                        sp = settle_pos.get(d)
                        if sp is not None and o["eng"] == "tensor" and not o["dma"]:
                            nd.add(sp)
                        else:
                            nd.add(remap[d])
                    o["deps"] = nd
                o["id"] = i
            self.ops = ops = newops
        for o in ops:
            kept = []
            for d in sorted(o["deps"]):
                dep = ops[d]
                if (not dep["dma"]) and (not o["dma"]) \
                        and dep["eng"] == o["eng"] and o["eng"] == "tensor":
                    continue
                kept.append(d)
                dep["inc"] = True
            o["kept"] = kept
        eng_count = {e: 0 for e in ENGINES}
        slot_count = [0] * self.n_dma_sems
        for o in ops:
            if not o["inc"]:
                o["sem"] = o["semval"] = None
                continue
            if o["dma"]:
                if o["pool"] == "s":
                    o["sem"] = ("dmas", o["slot"])
                    o["semval"] = 16
                else:
                    slot_count[o["slot"]] += 1
                    o["sem"] = ("dmah", o["slot"])
                    o["semval"] = 16 * slot_count[o["slot"]]
                o["incval"] = 16
            else:
                eng_count[o["eng"]] += 1
                o["sem"] = ("eng", o["eng"])
                o["semval"] = eng_count[o["eng"]]
                o["incval"] = 1
        for o in ops:
            o["waits"] = [(ops[d]["sem"], ops[d]["semval"]) for d in o["kept"]]

    def emit(self):
        nc = self.nc
        import contextlib
        stack = contextlib.ExitStack()
        sems = {}
        for e in ENGINES:
            sems[("eng", e)] = stack.enter_context(nc.semaphore(f"se_{e}"))
        for i in range(self.n_dma_sems):
            sems[("dmah", i)] = stack.enter_context(nc.semaphore(f"sh_{i}"))
            sems[("dmas", i)] = stack.enter_context(nc.semaphore(f"ss_{i}"))
        ops = self.ops
        with stack:
            with nc.Block() as blk:
                def section(eng_name):
                    def f(eng):
                        hw = {}
                        for o in ops:
                            if o["eng"] != eng_name:
                                continue
                            for sem_key, val in o["waits"]:
                                if sem_key[0] == "dmas":
                                    eng.wait_ge(sems[sem_key], val)
                                    continue
                                if hw.get(sem_key, 0) >= val:
                                    continue
                                hw[sem_key] = val
                                eng.wait_ge(sems[sem_key], val)
                            if o["dma"] and o["pool"] == "s" and o["reuse"]:
                                eng.sem_inc(sems[o["sem"]], -16)
                            inst = o["emit"](eng)
                            if o.get("settle") and self.settle_emit is not None:
                                inst = self.settle_emit(eng)
                            if o["inc"]:
                                inst.then_inc(sems[o["sem"]], o["incval"])
                    return f
                blk.tensor(section("tensor"))
                blk.scalar(section("scalar"))
                blk.vector(section("vector"))
                blk.gpsimd(section("gpsimd"))
                blk.sync(section("sync"))


def build(seq, dbg=False):
    nt = seq // 128
    nqt = seq // QTW
    nsl = seq // 512
    nft = seq // FT
    ntc = FT // 128
    nc = bass.Bass("TRN2", target_bir_lowering=False, debug=False)

    x_d = nc.dram_tensor("x", [seq, DM], F32, kind="ExternalInput")
    wq_d = nc.dram_tensor("wq", [8, 128, 8, 128], F32, kind="ExternalInput")
    wk_d = nc.dram_tensor("wk", [8, 128, 8, 128], F32, kind="ExternalInput")
    wv_d = nc.dram_tensor("wv", [8, 128, 8, 128], F32, kind="ExternalInput")
    wo_d = nc.dram_tensor("wo", [DM, DM], F32, kind="ExternalInput")
    bo_d = nc.dram_tensor("bo", [128, DM], F32, kind="ExternalInput")
    w1_d = nc.dram_tensor("w1", [32, 128, 8, 128], F32, kind="ExternalInput")
    w2_d = nc.dram_tensor("w2", [32, 128, 8, 128], F32, kind="ExternalInput")
    h1_d = nc.dram_tensor("h1", [32, 128, 32, 128], F32, kind="ExternalInput")
    h2_d = nc.dram_tensor("h2", [32, 128, 32, 128], F32, kind="ExternalInput")
    w3_d = nc.dram_tensor("w3", [8, 128, 32, 128], F32, kind="ExternalInput")
    s12_d = nc.dram_tensor("s12", [128, 128], F32, kind="ExternalInput")
    s3_d = nc.dram_tensor("s3", [128, DM], F32, kind="ExternalInput")
    id_d = nc.dram_tensor("idm", [128, 128], BF16, kind="ExternalInput")
    e2_d = nc.dram_tensor("e2", [128, 2], BF16, kind="ExternalInput")
    out_d = nc.dram_tensor("out", [seq, DM], F32, kind="ExternalOutput")
    x1_d = nc.dram_tensor("x1scr", [seq, DM], F32,
                          kind="ExternalOutput" if dbg else "Internal")
    nb_d = nc.dram_tensor("nbdbg", [NPAIR, 128], F32,
                          kind="ExternalOutput" if dbg else "Internal")
    nrm_d = nc.dram_tensor("nrmdbg", [NPAIR, 2, 24], F32,
                           kind="ExternalOutput" if dbg else "Internal")
    attn_d = nc.dram_tensor("attndbg", [128, 8 * seq], BF16,
                            kind="ExternalOutput" if dbg else "Internal")
    pvst_d = nc.dram_tensor("pvstdbg", [2, 128, 1024], F32,
                            kind="ExternalOutput" if dbg else "Internal")
    rbc_d = nc.dram_tensor("rbcdbg", [64, 512], F32,
                           kind="ExternalOutput" if dbg else "Internal")
    w1b_d = nc.dram_tensor("w1b", [32, 128, 1024], BF16, kind="Internal")
    w2b_d = nc.dram_tensor("w2b", [32, 128, 1024], BF16, kind="Internal")
    h1b_d = nc.dram_tensor("h1b", [32, 128, 4096], BF16, kind="Internal")
    h2b_d = nc.dram_tensor("h2b", [32, 128, 4096], BF16, kind="Internal")
    w3b_d = nc.dram_tensor("w3b", [8, 128, 4096], BF16, kind="Internal")

    RANGES = []

    def place(name, shape, dtype, at, regs=None):
        nbytes = int(np.prod(shape[1:])) * (2 if dtype == BF16 else 4)
        assert at % 32 == 0 and at >= SBUF_BASE
        assert at + nbytes <= BYTES_LIMIT, f"{name} overflows sbuf: {at + nbytes}"
        RANGES.append((at, at + nbytes, tuple(regs) if regs else (name,)))
        return nc.alloc_sbuf_tensor_at(name, list(shape), dtype, offset=at)

    class Z:
        def __init__(self, base):
            self.cur = base

        def take(self, name, shape, dtype, regs=None):
            at = self.cur
            nbytes = int(np.prod(shape[1:])) * (2 if dtype == BF16 else 4)
            self.cur = (at + nbytes + 31) & ~31
            return place(name, shape, dtype, at, regs=regs)

    zc = Z(SBUF_BASE)
    id_sb = zc.take("id_sb", [128, 128], BF16)
    e2_sb = zc.take("e2_sb", [128, 2], BF16)
    ones_sb = zc.take("ones_sb", [1, 128], BF16, regs=("ones",))
    cst = zc.take("cst", [128, 4], F32)
    s12_sb = zc.take("s12_sb", [128, 128], F32, regs=("s12",))
    s3_sb = zc.take("s3_sb", [128, DM], F32, regs=("s3",))
    bo_sb = zc.take("bo_sb", [128, DM], F32, regs=("bo",))
    SEG_A = zc.cur
    za1 = Z(SEG_A)
    attn_sb = za1.take("attn_sb", [128, 8 * seq], BF16, regs=("attn",))
    za2 = Z(SEG_A)
    x1f_sb = [za2.take(f"x1f{i}", [128, ntc * DM], F32) for i in range(2)]
    yt_sb = za2.take("yt", [128, ntc * DM], F32, regs=("yt",))
    xn2T_sb = za2.take("xn2T", [128, 8 * FT], BF16, regs=("xn2T",))
    ofb_sb = [za2.take(f"ofb{i}", [128, DM], F32) for i in range(2)]
    SEG_B = max(za1.cur, za2.cur)
    XN1REGS = tuple(f"xn1T{s}" for s in range(max(1, nsl)))
    zb1 = Z(SEG_B)
    xn1T_sb = zb1.take("xn1T", [128, 8 * seq], BF16, regs=XN1REGS)
    zb2 = Z(SEG_B)
    ab_sb = [zb2.take(f"ab{i}", [128, 32 * FT], BF16) for i in range(2)]
    wo_sb = place("wo_sb", [128, 8 * DM], BF16, SEG_B)
    SEG_C = max(zb1.cur, zb2.cur)
    z1a = Z(SEG_C)
    xa_sb = [z1a.take(f"xa{i}", [128, DM], F32) for i in range(2)]
    xnb_sb = [z1a.take(f"xnb{i}", [128, DM], BF16) for i in range(2)]
    z1b = Z(SEG_C)
    qt1_t = z1b.take("qt1", [128, seq], BF16)
    pbig_sb = [z1b.take("P0", [128, 1024], BF16, regs=("Ph0", "Ph1"))]
    q2_at = z1b.cur
    q2_sb = [z1b.take(f"q2_{i}", [128, 512], BF16) for i in range(2)]
    rhi2_sb = place("rhi2", [1, 512], BF16, q2_at, regs=("rhi2",))
    rlo2_sb = place("rlo2", [1, 512], BF16, q2_at + 1024, regs=("rlo2",))

    zl = Z(z1a.cur)
    st_sb = zl.take("st_sb", [128, 24], F32, regs=("st0", "st1"))
    mv_sb = [zl.take(f"mv{i}", [128, 2], F32) for i in range(2)]
    rs_sb = [zl.take(f"rs{i}", [128, 1], F32) for i in range(2)]
    A0_END = zl.cur
    z2a = Z(A0_END)
    assert z1b.cur <= A0_END, f"zone1 overflow: {z1b.cur} > {A0_END}"
    pbig_sb.append(z2a.take("P1", [128, 1024], BF16, regs=("P1h0", "P1h1")))
    rmx_sb = z2a.take("rmx", [2, 16], F32)
    nrm_sb = z2a.take("nrm", [2, 8], F32,
                      regs=tuple(f"nrm{i}" for i in range(6)))
    nbb_sb = z2a.take("nbb", [2, 1], BF16)
    nbr_sb = z2a.take("nbr", [1, 2], BF16, regs=("nbr", "nbr1"))
    negb_sb = [z2a.take(f"negb{i}", [128, 1], F32) for i in range(2)]
    rrow_sb = z2a.take("rrow", [1, 512], F32)
    rhi_sb = z2a.take("rhi", [1, 512], BF16)
    rlo_sb = z2a.take("rlo", [1, 512], BF16)
    rbc_sb = z2a.take("rbc", [64, 512], F32)
    qt_base = z2a.cur
    qt0_t = z2a.take("qt0", [128, seq], BF16)
    vext_sb = [z2a.take("vext0", [128, nt * 130], BF16)]
    qt_sb = [qt0_t, qt1_t]
    kt_sb = [z2a.take(f"kt{i}", [128, seq], BF16) for i in range(2)]
    wqs_sb = z2a.take("wqs", [128, 1024], BF16)
    wks_sb = z2a.take("wks", [128, 1024], BF16)
    wvs_sb = z2a.take("wvs", [128, 1024], BF16)
    cb_base = z2a.cur
    cbin_sb = [z2a.take(f"cbin{i}", [128, 256], F32) for i in range(2)]
    cbout_sb = [z2a.take(f"cbout{i}", [128, 256], BF16) for i in range(2)]
    assert z2a.cur <= BYTES_LIMIT, f"A1 overflow: {z2a.cur}"
    zfe = Z(qt_base)
    hs_sb = [zfe.take(f"hs{i}", [128, 4096], BF16) for i in range(2)]
    w12s_sb = [zfe.take(f"w12s{i}", [128, 1024], BF16) for i in range(4)]
    vext_sb.append(z2a.take("vext1", [128, nt * 130], BF16))
    assert z2a.cur <= BYTES_LIMIT, f"A1 overflow with vext1: {z2a.cur}"
    z2b = Z(zfe.cur)
    ot_sb = [z2b.take(f"ot{i}", [128, DM], F32) for i in range(2)]
    x1s_sb = [z2b.take(f"x1s{i}", [128, DM], F32) for i in range(2)]
    oln_sb = [z2b.take(f"oln{i}", [128, DM], F32) for i in range(2)]
    nmr_sb = [z2b.take(f"nmr{i}", [128, 1], F32) for i in range(2)]
    assert z2b.cur <= BYTES_LIMIT, f"A2 overflow: {z2b.cur}"
    zfs = Z(A0_END)
    sil_sb = [zfs.take(f"sil{i}", [128, 512], F32) for i in range(2)]
    zfl = Z(zfe.cur)
    w3q_sb = [zfl.take(f"w3q{i}", [128, 32 * 128], BF16) for i in range(2)]
    yl_sb = zfl.take("yl", [128, DM], F32)
    assert zfl.cur <= BYTES_LIMIT, f"FFN-late overflow: {zfl.cur}"

    OVERLAP = {}
    for i, (a0, e0, r0) in enumerate(RANGES):
        for (a1, e1, r1) in RANGES[i + 1:]:
            if a0 < e1 and a1 < e0:
                for r in r0:
                    OVERLAP.setdefault(r, set()).update(r1)
                for r in r1:
                    OVERLAP.setdefault(r, set()).update(r0)
    OVERLAP = {k: tuple(v - {k}) for k, v in OVERLAP.items()}

    pA = nc.alloc_psum_tensor("pA", [128, 1024], F32)
    pB = nc.alloc_psum_tensor("pB", [128, 1024], F32)
    pC = nc.alloc_psum_tensor("pC", [128, 1024], F32)
    pD = nc.alloc_psum_tensor("pD", [128, 1024], F32)
    PS = {"pA": pA, "pB": pB, "pC": pC, "pD": pD}

    def ph(name, half):
        return PS[name][:, half * 512:(half + 1) * 512], f"{name}{half}"

    def pbf(name):
        return PS[name].ap().bitcast(BF16)

    settle_sb = nc.alloc_sbuf_tensor_at("settle", [1, 16], F32, offset=229248)
    pb = PB(nc)
    pb.overlap = OVERLAP
    pb.settle_emit = lambda e: e.memset(settle_sb[0:1, 0:1], 0.0)

    pb.dma("sync", lambda e: e.dma_start(out=id_sb[:, :], in_=id_d.ap()), writes=("id_sb",))
    pb.dma("sync", lambda e: e.dma_start(out=e2_sb[:, :], in_=e2_d.ap()), writes=("e2_sb",))
    pb.dma("sync", lambda e: e.dma_start(out=s12_sb[:, :], in_=s12_d.ap()), writes=("s12",))
    pb.dma("sync", lambda e: e.dma_start(out=s3_sb[:, :], in_=s3_d.ap()), writes=("s3",))
    pb.dma("sync", lambda e: e.dma_start(out=bo_sb[:, :], in_=bo_d.ap()), writes=("bo",))
    pb.op("vector", lambda e: e.memset(ones_sb[:, :], 1.0), writes=("ones",))

    def set_cst(e):
        e.memset(cst[:, 0:1], EPS)
        e.memset(cst[:, 1:2], -0.5)
        e.memset(cst[:, 2:3], SCALE)
        return e.memset(cst[:, 3:4], 0.5)
    pb.op("vector", set_cst, writes=("cst",))

    def ln_chain(xr, xreg, b, outs):
        mv, rs = mv_sb[b], rs_sb[b]
        stc = st_sb[:, 12 * b:12 * b + 12]
        pb.op("vector", lambda e: e.bn_stats(stc[:, 0:6], xr[:, 0:512]),
              reads=(xreg,), writes=(f"st{b}",))
        pb.op("vector", lambda e: e.bn_stats(stc[:, 6:12], xr[:, 512:1024]),
              reads=(xreg, f"st{b}"), writes=(f"st{b}",))
        pb.op("vector", lambda e: e.bn_aggr(mv[:, :], stc[:, :]),
              reads=(f"st{b}",), writes=(f"mv{b}",))
        pb.op("scalar", lambda e: e.activation(rs[:, :], mv[:, 1:2], AF.Ln,
                                               bias=cst[:, 0:1], scale=1.0),
              reads=(f"mv{b}", "cst"), writes=(f"rs{b}",))
        pb.op("scalar", lambda e: e.activation(rs[:, :], rs[:, :], AF.Exp,
                                               bias=0.0, scale=cst[:, 1:2]),
              reads=(f"rs{b}", "cst"), writes=(f"rs{b}",))
        for out_ap, outreg in outs:
            pb.op("vector", lambda e, o=out_ap: e.tensor_scalar(
                o, xr[:, :], mv[:, 0:1], rs[:, 0:1], ALU.subtract, ALU.mult),
                reads=(xreg, f"mv{b}", f"rs{b}"), writes=(outreg,))

    def transpose8(src, srcreg, half, dst, dstreg):
        pv = pbf("pD")[:, 1024 * half:1024 * half + 1024]
        reg = f"pD{half}"
        for c in range(8):
            pb.op("tensor", lambda e, c=c: e.transpose(
                pv[:, c * 128:(c + 1) * 128], src[:, c * 128:(c + 1) * 128],
                id_sb[:, :]),
                reads=(srcreg, "id_sb"), writes=(reg,))
        pb.op("vector", lambda e: e.tensor_copy(
            dst, pv[:, 0:1024].rearrange("p (c s) -> p c s", c=8)),
            reads=(reg,), writes=(dstreg,))

    def emit_weight_casts():
        for (src_d, dst_d, nregs, tag) in (
                (w1_d, w1b_d, [f"w1b{o}" for o in range(32)], "w1"),
                (w2_d, w2b_d, [f"w2b{o}" for o in range(32)], "w2"),
                (h1_d, h1b_d, [f"h1b{o}" for o in range(32)], "h1"),
                (h2_d, h2b_d, [f"h2b{o}" for o in range(32)], "h2"),
                (w3_d, w3b_d, [f"w3b{o}" for o in range(8)], "w3")):
            pb.dma("gpsimd", lambda e, src_d=src_d, dst_d=dst_d: e.dma_start(
                out=dst_d.ap().rearrange("o p c -> (o p c)")
                if False else dst_d.ap(),
                in_=src_d.ap().rearrange("o p c x -> o p (c x)")),
                reads=("xa0", "xa1"), writes=tuple(nregs))

    tq = []

    def drain(k):
        for _ in range(min(k, len(tq))):
            tq.pop(0)()

    pdctr = [0]

    def next_pd():
        h = pdctr[0] % 2
        pdctr[0] += 1
        return PS["pD"][:, 512 * h:512 * (h + 1)], f"pD{h}"


    def make_prep(hp):
        qb = hp % 2
        qt, kt = qt_sb[qb], kt_sb[qb]
        vext = vext_sb[qb]
        vreg = f"vext{qb}"
        th = []
        for (dram, dst, dreg) in ((wq_d, wqs_sb, "wqs"), (wk_d, wks_sb, "wks"),
                                  (wv_d, wvs_sb, "wvs")):
            flat = dram.ap()[hp].rearrange("p c o -> p (c o)")
            for hh in range(4):
                def t_load(flat=flat, hh=hh, dst=dst, dreg=dreg):
                    b = hh % 2
                    pb.dma("sync", lambda e, flat=flat, hh=hh, b=b: e.dma_start(
                        out=cbin_sb[b][:, :],
                        in_=flat[:, 256 * hh:256 * (hh + 1)]),
                        writes=(f"cbin{b}",))
                    pb.op("vector", lambda e, hh=hh, b=b, dst=dst: e.tensor_copy(
                        dst[:, 256 * hh:256 * (hh + 1)], cbin_sb[b][:, :]),
                        reads=(f"cbin{b}",), writes=(dreg,))
                th.append(t_load)
        th.append(lambda vext=vext, vreg=vreg:
                  pb.op("vector", lambda e, vext=vext: e.memset(vext[:, :], 1.0),
                        writes=(vreg,)))
        projs = []
        for (wsl, wreg, dst, dreg, sc) in (
                (wqs_sb, "wqs", qt, f"qt{qb}", SCALE),
                (wks_sb, "wks", kt, f"kt{qb}", None)):
            lst = []
            for s in range(nsl):
                def t_proj(wsl=wsl, wreg=wreg, dst=dst, dreg=dreg, sc=sc, s=s):
                    pp, preg = next_pd()
                    for c in range(8):
                        pb.op("tensor", lambda e, pp=pp, wsl=wsl, c=c, s=s:
                              e.matmul(pp, wsl[:, c * 128:(c + 1) * 128],
                                       xn1T_r2[:, c, s * 512:(s + 1) * 512],
                                       start=(c == 0), stop=(c == 7)),
                              reads=(wreg, f"xn1T{s}"), writes=(preg,))
                    if sc is None:
                        pb.op("vector", lambda e, pp=pp, dst=dst, s=s:
                              e.tensor_copy(dst[:, s * 512:(s + 1) * 512], pp),
                              reads=(preg,), writes=(dreg,))
                    else:
                        pb.op("vector", lambda e, pp=pp, dst=dst, s=s:
                              e.tensor_scalar(dst[:, s * 512:(s + 1) * 512],
                                              pp, SCALE, None, ALU.mult),
                              reads=(preg,), writes=(dreg,))
                lst.append(t_proj)
            projs.append(lst)
        vext_r = vext[:, :].rearrange("p (t h v) -> p t h v", t=nt, h=2)
        tvs = []
        for tc in range(nt):
            def t_v(tc=tc, vext_r=vext_r, vreg=vreg):
                pp, preg = next_pd()
                for c in range(8):
                    pb.op("tensor", lambda e, pp=pp, c=c, tc=tc: e.matmul(
                        pp[:, 0:128],
                        xn1T_r2[:, c, tc * 128:(tc + 1) * 128],
                        wvs_sb[:, c * 128:(c + 1) * 128],
                        start=(c == 0), stop=(c == 7)),
                        reads=("wvs", f"xn1T{tc // 4}"), writes=(preg,))
                pb.op("vector", lambda e, pp=pp, tc=tc, vext_r=vext_r:
                      e.tensor_copy(vext_r[:, tc, :, 0:64],
                                    pp[:, 0:128].rearrange(
                                        "p (h v) -> p h v", v=64)),
                      reads=(preg, vreg), writes=(vreg,))
            tvs.append(t_v)
        norms = []
        reduces = []
        for (srcb, sreg, col) in ((qt, f"qt{qb}", 0), (kt, f"kt{qb}", 1)):
            nlst = []
            for s in range(nsl):
                def t_norm(srcb=srcb, sreg=sreg, s=s, col=col):
                    qp = s % 2
                    pb.op("vector", lambda e, srcb=srcb, s=s, qp=qp:
                          e.tensor_mul(q2_sb[qp][:, :],
                                       srcb[:, s * 512:(s + 1) * 512],
                                       srcb[:, s * 512:(s + 1) * 512]),
                          reads=(sreg,), writes=(f"q2_{qp}",))
                    np_, npreg = next_pd()
                    pb.op("tensor", lambda e, np_=np_, qp=qp: e.matmul(
                        np_[0:2, :], e2_sb[:, :], q2_sb[qp][:, :],
                        start=True, stop=True),
                        reads=(f"q2_{qp}", "e2_sb"), writes=(npreg,))
                    pb.op("vector", lambda e, np_=np_, s=s, col=col:
                          e.reduce_max(rmx_sb[:, col * 8 + s:col * 8 + s + 1],
                                       np_[0:2, :], axis=AX.X),
                          reads=(npreg, "rmx"), writes=("rmx",))
                nlst.append(t_norm)
            norms.append(nlst)
            reduces.append(lambda col=col:
                           pb.op("vector", lambda e, col=col: e.reduce_max(
                               nrm_sb[:, col:col + 1],
                               rmx_sb[:, col * 8:col * 8 + nsl], axis=AX.X),
                               reads=("rmx",), writes=(f"nrm{col}",)))
        for s in range(nsl):
            th.append(projs[0][s])
            th.append(projs[1][s])
            th.extend(tvs[4 * s:4 * s + 4])
        th.extend(norms[0])
        th.extend(norms[1])
        th.extend(reduces)
        th.append(lambda: pb.op("vector", lambda e: e.tensor_mul(
            nrm_sb[:, 2:3], nrm_sb[:, 0:1], nrm_sb[:, 1:2]),
            reads=("nrm0", "nrm1"), writes=("nrm2",)))
        th.append(lambda: pb.op("scalar", lambda e: e.activation(
            nrm_sb[:, 3:4], nrm_sb[:, 2:3], AF.Ln),
            reads=("nrm2",), writes=("nrm3",)))
        th.append(lambda: pb.op("scalar", lambda e: e.activation(
            nrm_sb[:, 4:5], nrm_sb[:, 3:4], AF.Exp, scale=cst[0:2, 3:4]),
            reads=("nrm3", "cst"), writes=("nrm4",)))
        th.append(lambda: pb.op("vector", lambda e: e.tensor_scalar(
            nrm_sb[:, 5:6], nrm_sb[:, 4:5], -1.0, None, ALU.mult),
            reads=("nrm4",), writes=("nrm5",)))
        th.append(lambda: pb.op("vector", lambda e: e.tensor_copy(
            nbb_sb[:, :], nrm_sb[:, 5:6]),
            reads=("nrm5",), writes=("nbb",)))

        def t_negb(qb=qb):
            hh = pdctr[0] % 2
            pdctr[0] += 1
            pv = pbf("pD")
            pb.op("tensor", lambda e, pv=pv, hh=hh: e.transpose(
                pv[0:1, 1024 * hh:1024 * hh + 2], nbb_sb[:, :], id_sb[0:2, 0:2]),
                reads=("nbb", "id_sb"), writes=(f"pD{hh}",))
            pb.op("vector", lambda e, pv=pv, hh=hh: e.tensor_copy(
                nbr_sb[:, :], pv[0:1, 1024 * hh:1024 * hh + 2]),
                reads=(f"pD{hh}",), writes=("nbr",))
            pb.op("vector", lambda e: e.tensor_reduce(
                nbr_sb[:, 0:1], nbr_sb[:, :], AX.X, ALU.min),
                reads=("nbr",), writes=("nbr1",))
            np_, npreg = next_pd()
            pb.op("tensor", lambda e, np_=np_: e.matmul(
                np_[:, 0:1], ones_sb[0:1, :], nbr_sb[:, 0:1],
                start=True, stop=True),
                reads=("nbr1", "ones"), writes=(npreg,))
            pb.op("vector", lambda e, np_=np_, qb=qb: e.tensor_copy(
                negb_sb[qb][:, :], np_[:, 0:1]),
                reads=(npreg,), writes=(f"negb{qb}",))
        th.append(t_negb)
        return th

    def make_wo_thunks():
        th = []
        wo_r8 = wo_d.ap().rearrange("(c p) o -> p c o", p=128)
        for c8 in range(32):
            def t_wo(c8=c8):
                b = c8 % 2
                pb.dma("sync", lambda e, c8=c8, b=b: e.dma_start(
                    out=cbin_sb[b][:, :],
                    in_=wo_r8[:, c8 // 4, 256 * (c8 % 4):256 * (c8 % 4) + 256]),
                    writes=(f"cbin{b}",))
                pb.op("vector", lambda e, c8=c8, b=b: e.tensor_copy(
                    wo_sb[:, c8 * 256:(c8 + 1) * 256], cbin_sb[b][:, :]),
                    reads=(f"cbin{b}",), writes=("wo_sb",))
            th.append(t_wo)
        return th

    xn1T_r = xn1T_sb[:, :].rearrange("p (c s) -> p c s", c=8)
    xn1T_r2 = xn1T_r
    th0 = make_prep(0)
    for t_ in th0[:13]:
        t_()
    for t in range(nt):
        b = t % 2
        xa = xa_sb[b]
        for qp in range(4):
            pb.dma("sync", lambda e, t=t, xa=xa, qp=qp: e.dma_start(
                out=xa[32 * qp:32 * qp + 32, :],
                in_=x_d.ap()[128 * t + 32 * qp:128 * t + 32 * qp + 32, :]),
                writes=(f"xa{b}",))
        ln_chain(xa[:, :], f"xa{b}", b, [(xnb_sb[b][:, :], f"xnb{b}")])
        transpose8(xnb_sb[b], f"xnb{b}", b,
                   xn1T_r[:, :, t * 128:(t + 1) * 128], f"xn1T{t // 4}")
        if t % 4 == 3:
            for t_ in th0[13 + 6 * (t // 4):13 + 6 * (t // 4 + 1)]:
                t_()
    for t_ in th0[13 + 6 * nsl:]:
        t_()

    emit_weight_casts()

    for hp in range(NPAIR):
        qb = hp % 2
        qt, kt = qt_sb[qb], kt_sb[qb]
        vext_l = vext_sb[qb][:, :].rearrange("p (t y) -> p t y", y=130)
        negb = negb_sb[qb]
        tq.extend(make_prep(hp + 1) if hp + 1 < NPAIR else make_wo_thunks())
        rate = max(1, -(-len(tq) // max(1, nqt * nt - 8)))
        for q in range(nqt):
            PKEYS = (("Ph0", "Ph1"), ("P1h0", "P1h1"))

            def emit_pv(c):
                sb = c % 2
                pbt = pbig_sb[sb]
                for a in range(2):
                    pvx, pvreg = ph("pC", a)
                    pb.op("tensor", lambda e, pvx=pvx, c=c, a=a, pbt=pbt,
                          vext_l=vext_l: e.matmul(
                        pvx[0:65, :], vext_l[:, c, 65 * a:65 * a + 65],
                        pbt[:, 512 * a:512 * (a + 1)],
                        start=(c == 0), stop=(c == nt - 1)),
                        reads=(PKEYS[sb][a], f"vext{qb}"), writes=(pvreg,))
            for c in range(nt):
                sb = c % 2
                pn = "pA" if sb == 0 else "pB"
                sa, sareg = ph(pn, 0)
                sbb, sbreg = ph(pn, 1)
                pb.op("tensor", lambda e, sa=sa, c=c, q=q, kt=kt, qt=qt: e.matmul(
                    sa, kt[0:64, c * 128:(c + 1) * 128],
                    qt[0:64, q * QTW:(q + 1) * QTW], start=True, stop=True),
                    reads=(f"kt{qb}", f"qt{qb}"), writes=(sareg,))
                pb.op("tensor", lambda e, sbb=sbb, c=c, q=q, kt=kt, qt=qt: e.matmul(
                    sbb, kt[64:128, c * 128:(c + 1) * 128],
                    qt[64:128, q * QTW:(q + 1) * QTW], start=True, stop=True),
                    reads=(f"kt{qb}", f"qt{qb}"), writes=(sbreg,))
                pb.op("scalar", lambda e, pn=pn, sb=sb, negb=negb: e.activation(
                    pbig_sb[sb][:, :], PS[pn][:, 0:1024], AF.Exp,
                    bias=negb[:, 0:1]),
                    reads=(sareg, sbreg, f"negb{qb}"),
                    writes=PKEYS[sb])
                if c > 0:
                    emit_pv(c - 1)
                if c < nt - 3:
                    drain(rate)
            emit_pv(nt - 1)
            sets = (((rrow_sb, rhi_sb, rlo_sb), ("rrow", "rhi", "rlo")),
                    ((rrow_sb, rhi2_sb, rlo2_sb), ("rrow", "rhi2", "rlo2")))
            for a, (bufs, regs) in enumerate(sets):
                pvx, pvreg = ph("pC", a)
                ro, hi, lo = bufs
                r0, r1, r2 = regs
                pb.op("vector", lambda e, pvx=pvx, ro=ro: e.reciprocal(
                    ro[:, :], pvx[64:65, :]),
                    reads=(pvreg,), writes=(r0,))
                pb.op("vector", lambda e, ro=ro, hi=hi: e.tensor_copy(
                    hi[:, :], ro[:, :]),
                    reads=(r0,), writes=(r1,))
                pb.op("vector", lambda e, ro=ro, hi=hi, lo=lo: e.tensor_sub(
                    lo[:, :], ro[:, :], hi[:, :]),
                    reads=(r0, r1), writes=(r2,))
            bcs = []
            for a, (bufs, regs) in enumerate(sets):
                ro, hi, lo = bufs
                r0, r1, r2 = regs
                bc, bcreg = next_pd()
                bcs.append((bc, bcreg))
                pb.op("tensor", lambda e, bc=bc, hi=hi: e.matmul(
                    bc[0:64, :], ones_sb[0:1, 0:64], hi[:, :],
                    start=True, stop=False),
                    reads=(r1, "ones"), writes=(bcreg,))
                pb.op("tensor", lambda e, bc=bc, lo=lo: e.matmul(
                    bc[0:64, :], ones_sb[0:1, 0:64], lo[:, :],
                    start=False, stop=True),
                    reads=(r2, "ones"), writes=(bcreg,))
            for a in range(2):
                pvx, pvreg = ph("pC", a)
                bc, bcreg = bcs[a]
                pb.op("vector", lambda e, bc=bc: e.tensor_copy(
                    rbc_sb[:, :], bc[0:64, :]),
                    reads=(bcreg, "rbc"), writes=("rbc",))
                pb.op("vector", lambda e, pvx=pvx, hp=hp, a=a, q=q: e.tensor_mul(
                    attn_sb[64 * a:64 * a + 64,
                            hp * seq + q * QTW:hp * seq + (q + 1) * QTW],
                    pvx[0:64, :], rbc_sb[:, :]),
                    reads=(pvreg, "rbc"), writes=("attn",))
            drain(3 * rate)
        drain(len(tq))

    def load_fslab(dram, dst, dstreg, idx, c=8, srcreg=None):
        pb.dma("sync", lambda e: e.dma_start(out=dst[:, :], in_=dram.ap()[idx]),
               reads=(srcreg,) if srcreg else (), writes=(dstreg,))

    def preload_s1():
        load_fslab(w1b_d, w12s_sb[0], "w12s0", 0, srcreg="w1b0")
        load_fslab(w2b_d, w12s_sb[2], "w12s2", 0, srcreg="w2b0")
        load_fslab(w1b_d, w12s_sb[1], "w12s1", 1, srcreg="w1b1")
        load_fslab(w2b_d, w12s_sb[3], "w12s3", 1, srcreg="w2b1")

    attn_r = attn_sb[:, :].rearrange("p (c s) -> p c s", c=8)
    wo_r = wo_sb[:, :].rearrange("p (c o) -> p c o", c=8)
    for t in range(nt):
        b = t % 2
        pn = "pA" if b == 0 else "pB"
        xs = xa_sb[b]
        for qp in range(4):
            pb.dma("sync", lambda e, t=t, xs=xs, qp=qp: e.dma_start(
                out=xs[32 * qp:32 * qp + 32, :],
                in_=x_d.ap()[128 * t + 32 * qp:128 * t + 32 * qp + 32, :]),
                writes=(f"xa{b}",))
        for h in range(2):
            po, poreg = ph(pn, h)
            for c in range(8):
                pb.op("tensor", lambda e, po=po, c=c, h=h, t=t: e.matmul(
                    po, attn_r[:, c, t * 128:(t + 1) * 128],
                    wo_r[:, c, h * 512:(h + 1) * 512],
                    start=(c == 0), stop=(c == 7)),
                    reads=("attn", "wo_sb"), writes=(poreg,))
        pb.op("vector", lambda e, pn=pn, b=b: e.tensor_add(
            ot_sb[b][:, :], PS[pn][:, :], bo_sb[:, :]),
            reads=(pn + "0", pn + "1", "bo"), writes=(f"ot{b}",))
        ot = ot_sb[b]
        stc = st_sb[:, 12 * b:12 * b + 12]
        mv, rs = mv_sb[b], rs_sb[b]
        pb.op("vector", lambda e, ot=ot, stc=stc: e.bn_stats(stc[:, 0:6], ot[:, 0:512]),
              reads=(f"ot{b}",), writes=(f"st{b}",))
        pb.op("vector", lambda e, ot=ot, stc=stc: e.bn_stats(stc[:, 6:12], ot[:, 512:1024]),
              reads=(f"ot{b}", f"st{b}"), writes=(f"st{b}",))
        pb.op("vector", lambda e, mv=mv, stc=stc: e.bn_aggr(mv[:, :], stc[:, :]),
              reads=(f"st{b}",), writes=(f"mv{b}",))
        pb.op("scalar", lambda e, mv=mv, rs=rs: e.activation(
            rs[:, :], mv[:, 1:2], AF.Ln, bias=cst[:, 0:1], scale=1.0),
            reads=(f"mv{b}", "cst"), writes=(f"rs{b}",))
        pb.op("scalar", lambda e, rs=rs: e.activation(
            rs[:, :], rs[:, :], AF.Exp, bias=0.0, scale=cst[:, 1:2]),
            reads=(f"rs{b}", "cst"), writes=(f"rs{b}",))
        pb.op("vector", lambda e, mv=mv, rs=rs, b=b: e.tensor_scalar(
            nmr_sb[b][:, :], mv[:, 0:1], rs[:, 0:1], -1.0, ALU.mult, ALU.mult),
            reads=(f"mv{b}", f"rs{b}"), writes=(f"nmr{b}",))
        pb.op("scalar", lambda e, ot=ot, rs=rs, b=b: e.activation(
            oln_sb[b][:, :], ot[:, :], AF.Identity, bias=nmr_sb[b][:, 0:1],
            scale=rs[:, 0:1]),
            reads=(f"ot{b}", f"rs{b}", f"nmr{b}"), writes=(f"oln{b}",))
        pb.op("vector", lambda e, b=b: e.tensor_add(
            x1s_sb[b][:, :], oln_sb[b][:, :], xa_sb[b][:, :]),
            reads=(f"oln{b}", f"xa{b}"), writes=(f"x1s{b}",))
        pb.dma("sync", lambda e, t=t, b=b: e.dma_start(
            out=x1_d.ap()[128 * t:128 * (t + 1), :], in_=x1s_sb[b][:, :]),
            reads=(f"x1s{b}",), writes=(f"x1d{t // ntc}",))
        if t == max(3, nt - 4):
            preload_s1()
            load_fslab(h1b_d, hs_sb[0], "hs0", 0, srcreg="h1b0")
            load_fslab(h1b_d, hs_sb[1], "hs1", 1, srcreg="h1b1")

    s1c, s2c, sh1c, sh2c = (s12_sb[:, 0:32], s12_sb[:, 32:64],
                            s12_sb[:, 64:96], s12_sb[:, 96:128])
    xn2T_r = xn2T_sb[:, :].rearrange("p (c s) -> p c s", c=8)

    def emit_prologue(ft):
        fb = ft % 2
        x1f = x1f_sb[fb]
        pb.dma("sync", lambda e, ft=ft, x1f=x1f: e.dma_start(
            out=x1f[:, :].rearrange("p (n d) -> p n d", n=ntc),
            in_=x1_d.ap()[ft * FT:(ft + 1) * FT, :].rearrange(
                "(n p) d -> p n d", p=128)),
            reads=(f"x1d{ft}",), writes=(f"x1f{fb}",))
        for tc in range(ntc):
            b = tc % 2
            ln_chain(x1f[:, tc * DM:(tc + 1) * DM], f"x1f{fb}", b,
                     [(xnb_sb[b][:, :], f"xnb{b}")])
            transpose8(xnb_sb[b], f"xnb{b}", b,
                       xn2T_r[:, :, tc * 128:(tc + 1) * 128], "xn2T")

    emit_prologue(0)
    for ft in range(nft):
        fb = ft % 2
        x1f = x1f_sb[fb]
        if ft == 0:
            load_fslab(w3b_d, w3q_sb[0], "w3q0", 0, srcreg="w3b0")
            load_fslab(w3b_d, w3q_sb[1], "w3q1", 1, srcreg="w3b1")
        for o in range(32):
            b = o % 2
            pn = "pA" if b == 0 else "pB"
            u1, u1reg = ph(pn, 0)
            u2, u2reg = ph(pn, 1)
            w1s, w2s = w12s_sb[b], w12s_sb[2 + b]
            for c in range(8):
                pb.op("tensor", lambda e, u1=u1, w1s=w1s, c=c: e.matmul(
                    u1, w1s[:, c * 128:(c + 1) * 128], xn2T_r[:, c, :],
                    start=(c == 0), stop=(c == 7)),
                    reads=(f"w12s{b}", "xn2T"), writes=(u1reg,))
            for c in range(8):
                pb.op("tensor", lambda e, u2=u2, w2s=w2s, c=c: e.matmul(
                    u2, w2s[:, c * 128:(c + 1) * 128], xn2T_r[:, c, :],
                    start=(c == 0), stop=(c == 7)),
                    reads=(f"w12s{2 + b}", "xn2T"), writes=(u2reg,))
            pb.op("scalar", lambda e, u1=u1, o=o, b=b: e.activation(
                sil_sb[b][:, :], u1, AF.Silu, scale=s1c[:, o:o + 1]),
                reads=(u1reg, "s12"), writes=(f"sil{b}",))
            pb.op("vector", lambda e, u2=u2, o=o, b=b: e.scalar_tensor_tensor(
                ab_sb[0][:, o * 512:(o + 1) * 512], u2, s2c[:, o:o + 1],
                sil_sb[b][:, :], ALU.mult, ALU.mult),
                reads=(u2reg, "s12", f"sil{b}"), writes=("ab0",))
            if o + 2 < 32:
                load_fslab(w1b_d, w12s_sb[b], f"w12s{b}", o + 2, srcreg=f"w1b{o + 2}")
                load_fslab(w2b_d, w12s_sb[2 + b], f"w12s{2 + b}", o + 2, srcreg=f"w2b{o + 2}")
        if ft + 1 < nft:
            emit_prologue(ft + 1)
            preload_s1()
        for (hd_, hname, src, sreg, dst, dreg, sc, nxt) in (
                (h1b_d, "h1b", ab_sb[0], "ab0", ab_sb[1], "ab1", sh1c,
                 (h2b_d, "h2b")),
                (h2b_d, "h2b", ab_sb[1], "ab1", ab_sb[0], "ab0", sh2c,
                 (h1b_d, "h1b") if ft + 1 < nft else None)):
            src_r = src[:, :].rearrange("p (c s) -> p c s", c=32)
            for o in range(32):
                b = o % 2
                pn = "pA" if b == 0 else "pB"
                uu, ureg = ph(pn, 0)
                hsl = hs_sb[b]
                for c in range(32):
                    pb.op("tensor", lambda e, uu=uu, hsl=hsl, c=c, src_r=src_r:
                          e.matmul(uu, hsl[:, c * 128:(c + 1) * 128], src_r[:, c, :],
                                   start=(c == 0), stop=(c == 31)),
                          reads=(f"hs{b}", sreg), writes=(ureg,))
                pb.op("scalar", lambda e, uu=uu, o=o, dst=dst, sc=sc: e.activation(
                    dst[:, o * 512:(o + 1) * 512], uu, AF.Gelu,
                    scale=sc[:, o:o + 1]),
                    reads=(ureg, "s12"), writes=(dreg,))
                if o + 2 < 32:
                    load_fslab(hd_, hs_sb[b], f"hs{b}", o + 2, srcreg=f"{hname}{o + 2}")
                elif nxt is not None:
                    load_fslab(nxt[0], hs_sb[b], f"hs{b}", o - 30,
                               srcreg=f"{nxt[1]}{o - 30}")
        cT_r = ab_sb[0][:, :].rearrange("p (c s) -> p c s", c=32)
        for qd in range(8):
            b = qd % 2
            w3s = w3q_sb[b]
            for tc in range(ntc):
                yy, yreg = ph("pC", tc % 2)
                for c in range(32):
                    pb.op("tensor", lambda e, yy=yy, w3s=w3s, c=c, tc=tc: e.matmul(
                        yy[:, 0:128], cT_r[:, c, tc * 128:(tc + 1) * 128],
                        w3s[:, c * 128:(c + 1) * 128],
                        start=(c == 0), stop=(c == 31)),
                        reads=(f"w3q{b}", "ab0"), writes=(yreg,))
                pb.op("vector", lambda e, yy=yy, tc=tc, qd=qd: e.tensor_mul(
                    yt_sb[:, tc * DM + qd * 128:tc * DM + (qd + 1) * 128],
                    yy[:, 0:128], s3_sb[:, qd * 128:(qd + 1) * 128]),
                    reads=(yreg, "s3"), writes=("yt",))
            if qd + 2 < 8:
                load_fslab(w3b_d, w3q_sb[b], f"w3q{b}", qd + 2, srcreg=f"w3b{qd + 2}")
            elif ft + 1 < nft:
                load_fslab(w3b_d, w3q_sb[b], f"w3q{b}", qd - 6, srcreg=f"w3b{qd - 6}")
        for tc in range(ntc):
            b = tc % 2
            ln_chain(yt_sb[:, tc * DM:(tc + 1) * DM], "yt", b,
                     [(yl_sb[:, :], "yl")])
            pb.op("vector", lambda e, b=b, tc=tc, x1f=x1f: e.tensor_add(
                ofb_sb[b][:, :], yl_sb[:, :], x1f[:, tc * DM:(tc + 1) * DM]),
                reads=("yl", f"x1f{fb}"), writes=(f"ofb{b}",))
            pb.dma("sync", lambda e, ft=ft, tc=tc, b=b: e.dma_start(
                out=out_d.ap()[ft * FT + tc * 128:ft * FT + (tc + 1) * 128, :],
                in_=ofb_sb[b][:, :]),
                reads=(f"ofb{b}",), writes=("outd",))

    pb.barrier()
    pb.build()
    pb.emit()
    nc.finalize()
    return nc


def host_prep(inputs, seq=4096):
    bf = ml_dtypes.bfloat16
    f32 = np.float32

    def cc(a):
        return np.ascontiguousarray(a, dtype=f32)

    Wq = np.asarray(inputs["L0_attn_Wq"], dtype=f32)
    Wk = np.asarray(inputs["L0_attn_Wk"], dtype=f32)
    Wv = np.asarray(inputs["L0_attn_Wv"], dtype=f32)
    Wo = np.asarray(inputs["L0_attn_Wo"], dtype=f32)
    bo = np.asarray(inputs["L0_attn_bo"], dtype=f32)
    w1 = np.asarray(inputs["L0_ffn_w1_w"], dtype=f32)
    w2 = np.asarray(inputs["L0_ffn_w2_w"], dtype=f32)
    h1 = np.asarray(inputs["L0_ffn_h1_w"], dtype=f32)
    h2 = np.asarray(inputs["L0_ffn_h2_w"], dtype=f32)
    w3 = np.asarray(inputs["L0_ffn_w3_w"], dtype=f32)
    s1 = np.asarray(inputs["L0_ffn_w1_s"], dtype=f32)
    s2 = np.asarray(inputs["L0_ffn_w2_s"], dtype=f32)
    sh1 = np.asarray(inputs["L0_ffn_h1_s"], dtype=f32)
    sh2 = np.asarray(inputs["L0_ffn_h2_s"], dtype=f32)
    s3 = np.asarray(inputs["L0_ffn_w3_s"], dtype=f32)

    def pack_qkv(W):
        return cc(W.T.reshape(8, 128, 8, 128).transpose(2, 1, 0, 3))

    def pack_ffn(W, oc, och):
        din = W.shape[1]
        return cc(W.T.reshape(din // 128, 128, oc, och).transpose(2, 1, 0, 3))

    s12 = np.zeros((128, 128), dtype=f32)
    s12[:, 0:32] = s1.reshape(32, 128).T
    s12[:, 32:64] = s2.reshape(32, 128).T
    s12[:, 64:96] = sh1.reshape(32, 128).T
    s12[:, 96:128] = sh2.reshape(32, 128).T

    e2 = np.zeros((128, 2), dtype=bf)
    e2[0:64, 0] = 1
    e2[64:128, 1] = 1

    shared = {
        "wq": pack_qkv(Wq), "wk": pack_qkv(Wk), "wv": pack_qkv(Wv),
        "wo": cc(Wo.T),
        "bo": cc(np.broadcast_to(bo, (128, DM))),
        "w1": pack_ffn(w1, 32, 128), "w2": pack_ffn(w2, 32, 128),
        "h1": pack_ffn(h1, 32, 128), "h2": pack_ffn(h2, 32, 128),
        "w3": pack_ffn(w3, 8, 128),
        "s12": s12,
        "s3": cc(np.broadcast_to(s3, (128, DM))),
        "idm": np.eye(128, dtype=bf),
        "e2": e2,
    }
    x = np.asarray(inputs["L0_attn_x"], dtype=f32)
    in_maps = []
    for b in range(x.shape[0]):
        m = dict(shared)
        m["x"] = cc(x[b][:seq])
        in_maps.append(m)
    return in_maps


def kernel(**inputs):
    seq = 4096
    in_maps = host_prep(inputs, seq)
    nc = build(seq)
    res = run_bass_kernel_spmd(nc, in_maps, core_ids=list(range(8)))
    return np.stack([r["out"] for r in res.results], axis=0)


# version 17
# speedup vs baseline: 1.0947x; 1.0947x over previous
import numpy as np
import ml_dtypes
import concourse.bass as bass
import concourse.mybir as mybir
from concourse.bass_utils import run_bass_kernel_spmd

F32 = mybir.dt.float32
BF16 = mybir.dt.bfloat16
AF = mybir.ActivationFunctionType
ALU = mybir.AluOpType
AX = mybir.AxisListType

DM = 1024
HEADS = 16
NPAIR = 8
SCALE = (DM // HEADS) ** -0.5
EPS = 1e-5
QTW = 512
FT = 512
SBUF_BASE = 16512
BYTES_LIMIT = 229248

ENGINES = ("tensor", "scalar", "vector", "gpsimd", "sync")


class PB:

    def __init__(self, nc, n_dma_sems=16):
        self.nc = nc
        self.n_dma_sems = n_dma_sems
        self.ops = []
        self.lw = {}
        self.rd = {}
        self.rings = {"h": [None] * n_dma_sems, "s": [None] * n_dma_sems}
        self.counts = {"h": 0, "s": 0}
        self.overlap = {}
        self.settle_emit = None

    def op(self, eng, emit, reads=(), writes=(), dma=False, forced=(),
           nmulti=1):
        oid = len(self.ops)
        if self.overlap:
            ext = []
            for w in writes:
                ext.extend(self.overlap.get(w, ()))
            if ext:
                writes = tuple(writes) + tuple(ext)
        deps = set()
        for r in reads:
            w = self.lw.get(r)
            if w is not None:
                deps.add(w)
        for w in writes:
            p = self.lw.get(w)
            if p is not None:
                deps.add(p)
            deps.update(self.rd.get(w, ()))
        deps.update(forced)
        slot = pool = None
        reuse = False
        if dma:
            pool = "s" if eng == "gpsimd" else "h"
            slot = self.counts[pool] % self.n_dma_sems
            prev = self.rings[pool][slot]
            if prev is not None:
                deps.add(prev)
                deps.update(self.ops[prev].get("consumers", ()))
                reuse = True
            self.rings[pool][slot] = oid
            self.counts[pool] += 1
        deps.discard(oid)
        for r in reads:
            self.rd.setdefault(r, set()).add(oid)
            self.ops[self.lw[r]].setdefault("consumers", set()).add(oid) \
                if r in self.lw else None
        for w in writes:
            self.lw[w] = oid
            self.rd[w] = set()
        self.ops.append({"id": oid, "eng": eng, "emit": emit, "deps": deps,
                         "dma": dma, "slot": slot, "pool": pool, "inc": dma,
                         "reuse": reuse, "nmulti": nmulti})
        return oid

    def dma(self, eng, emit, reads=(), writes=()):
        return self.op(eng, emit, reads, writes, dma=True)

    def dma_chunks(self, eng, emits, reads=(), writes=()):
        return self.op(eng, emits, reads, writes, dma=True,
                       nmulti=len(emits))

    def barrier(self):
        allops = set(self.lw.values())
        for s in self.rd.values():
            allops |= s
        fr = frozenset(allops)
        for e in ENGINES:
            self.op(e, lambda eng: eng.nop(), forced=fr)

    def build(self):
        ops = self.ops
        needs = set()
        for o in ops:
            if o["eng"] != "tensor" or o["dma"]:
                continue
            for d in o["deps"]:
                dep = ops[d]
                if dep["eng"] == "vector" and not dep["dma"]:
                    needs.add(d)
        if needs and self.settle_emit is not None:
            newops = []
            remap = {}
            settle_pos = {}
            prev_sp = None
            for o in ops:
                remap[o["id"]] = len(newops)
                newops.append(o)
                if o["id"] in needs:
                    fd = {len(newops) - 1}
                    if prev_sp is not None:
                        fd.add(prev_sp)
                    newops.append({"eng": "vector", "emit": self.settle_emit,
                                   "fdeps": fd, "dma": False, "slot": None,
                                   "pool": None, "inc": False, "reuse": False})
                    settle_pos[o["id"]] = len(newops) - 1
                    prev_sp = len(newops) - 1
            for i, o in enumerate(newops):
                if "fdeps" in o:
                    o["deps"] = set(o.pop("fdeps"))
                else:
                    nd = set()
                    for d in o["deps"]:
                        sp = settle_pos.get(d)
                        if sp is not None and o["eng"] == "tensor" and not o["dma"]:
                            nd.add(sp)
                        else:
                            nd.add(remap[d])
                    o["deps"] = nd
                o["id"] = i
            self.ops = ops = newops
        for o in ops:
            kept = []
            for d in sorted(o["deps"]):
                dep = ops[d]
                if (not dep["dma"]) and (not o["dma"]) \
                        and dep["eng"] == o["eng"] and o["eng"] == "tensor":
                    continue
                kept.append(d)
                dep["inc"] = True
            o["kept"] = kept
        eng_count = {e: 0 for e in ENGINES}
        slot_count = [0] * self.n_dma_sems
        for o in ops:
            if not o["inc"]:
                o["sem"] = o["semval"] = None
                continue
            if o["dma"]:
                if o["pool"] == "s":
                    o["sem"] = ("dmas", o["slot"])
                    o["semval"] = 16 * o.get("nmulti", 1)
                else:
                    slot_count[o["slot"]] += 1
                    o["sem"] = ("dmah", o["slot"])
                    o["semval"] = 16 * slot_count[o["slot"]]
                o["incval"] = 16
            else:
                eng_count[o["eng"]] += 1
                o["sem"] = ("eng", o["eng"])
                o["semval"] = eng_count[o["eng"]]
                o["incval"] = 1
        for o in ops:
            o["waits"] = [(ops[d]["sem"], ops[d]["semval"]) for d in o["kept"]]

    def emit(self):
        nc = self.nc
        import contextlib
        stack = contextlib.ExitStack()
        sems = {}
        for e in ENGINES:
            sems[("eng", e)] = stack.enter_context(nc.semaphore(f"se_{e}"))
        for i in range(self.n_dma_sems):
            sems[("dmah", i)] = stack.enter_context(nc.semaphore(f"sh_{i}"))
            sems[("dmas", i)] = stack.enter_context(nc.semaphore(f"ss_{i}"))
        ops = self.ops
        with stack:
            with nc.Block() as blk:
                def section(eng_name):
                    def f(eng):
                        hw = {}
                        for o in ops:
                            if o["eng"] != eng_name:
                                continue
                            for sem_key, val in o["waits"]:
                                if sem_key[0] == "dmas":
                                    eng.wait_ge(sems[sem_key], val)
                                    continue
                                if hw.get(sem_key, 0) >= val:
                                    continue
                                hw[sem_key] = val
                                eng.wait_ge(sems[sem_key], val)
                            if o["dma"] and o["pool"] == "s" and o["reuse"]:
                                eng.sem_inc(sems[o["sem"]], -16)
                            if isinstance(o["emit"], list):
                                for sub in o["emit"]:
                                    sub(eng).then_inc(sems[o["sem"]],
                                                      o["incval"])
                                continue
                            inst = o["emit"](eng)
                            if o.get("settle") and self.settle_emit is not None:
                                inst = self.settle_emit(eng)
                            if o["inc"]:
                                inst.then_inc(sems[o["sem"]], o["incval"])
                    return f
                blk.tensor(section("tensor"))
                blk.scalar(section("scalar"))
                blk.vector(section("vector"))
                blk.gpsimd(section("gpsimd"))
                blk.sync(section("sync"))


def build(seq, dbg=False):
    nt = seq // 128
    nqt = seq // QTW
    nsl = seq // 512
    nft = seq // FT
    ntc = FT // 128
    nc = bass.Bass("TRN2", target_bir_lowering=False, debug=False)

    x_d = nc.dram_tensor("x", [seq, DM], F32, kind="ExternalInput")
    qkv_d = nc.dram_tensor("qkv", [3, 8, 128, 8, 128], F32,
                           kind="ExternalInput")
    wo_d = nc.dram_tensor("wo", [DM, DM], F32, kind="ExternalInput")
    qkvb_d = nc.dram_tensor("qkvb", [3, 8, 128, 1024], BF16, kind="Internal")
    wob_d = nc.dram_tensor("wob", [128, 8, 1024], BF16, kind="Internal")
    bo_d = nc.dram_tensor("bo", [128, DM], F32, kind="ExternalInput")
    w1_d = nc.dram_tensor("w1", [32, 128, 8, 128], F32, kind="ExternalInput")
    w2_d = nc.dram_tensor("w2", [32, 128, 8, 128], F32, kind="ExternalInput")
    h1_d = nc.dram_tensor("h1", [32, 128, 32, 128], F32, kind="ExternalInput")
    h2_d = nc.dram_tensor("h2", [32, 128, 32, 128], F32, kind="ExternalInput")
    w3_d = nc.dram_tensor("w3", [8, 128, 32, 128], F32, kind="ExternalInput")
    s12_d = nc.dram_tensor("s12", [128, 128], F32, kind="ExternalInput")
    s3_d = nc.dram_tensor("s3", [128, DM], F32, kind="ExternalInput")
    id_d = nc.dram_tensor("idm", [128, 128], BF16, kind="ExternalInput")
    e2_d = nc.dram_tensor("e2", [128, 2], BF16, kind="ExternalInput")
    out_d = nc.dram_tensor("out", [seq, DM], F32, kind="ExternalOutput")
    x1_d = nc.dram_tensor("x1scr", [seq, DM], F32,
                          kind="ExternalOutput" if dbg else "Internal")
    nb_d = nc.dram_tensor("nbdbg", [NPAIR, 128], F32,
                          kind="ExternalOutput" if dbg else "Internal")
    nrm_d = nc.dram_tensor("nrmdbg", [NPAIR, 2, 24], F32,
                           kind="ExternalOutput" if dbg else "Internal")
    attn_d = nc.dram_tensor("attndbg", [128, 8 * seq], BF16,
                            kind="ExternalOutput" if dbg else "Internal")
    pvst_d = nc.dram_tensor("pvstdbg", [2, 128, 1024], F32,
                            kind="ExternalOutput" if dbg else "Internal")
    rbc_d = nc.dram_tensor("rbcdbg", [64, 512], F32,
                           kind="ExternalOutput" if dbg else "Internal")
    w1b_d = nc.dram_tensor("w1b", [32, 128, 1024], BF16, kind="Internal")
    w2b_d = nc.dram_tensor("w2b", [32, 128, 1024], BF16, kind="Internal")
    h1b_d = nc.dram_tensor("h1b", [32, 128, 4096], BF16, kind="Internal")
    h2b_d = nc.dram_tensor("h2b", [32, 128, 4096], BF16, kind="Internal")
    w3b_d = nc.dram_tensor("w3b", [8, 128, 4096], BF16, kind="Internal")

    RANGES = []

    def place(name, shape, dtype, at, regs=None):
        nbytes = int(np.prod(shape[1:])) * (2 if dtype == BF16 else 4)
        assert at % 32 == 0 and at >= SBUF_BASE
        assert at + nbytes <= BYTES_LIMIT, f"{name} overflows sbuf: {at + nbytes}"
        RANGES.append((at, at + nbytes, tuple(regs) if regs else (name,)))
        return nc.alloc_sbuf_tensor_at(name, list(shape), dtype, offset=at)

    class Z:
        def __init__(self, base):
            self.cur = base

        def take(self, name, shape, dtype, regs=None):
            at = self.cur
            nbytes = int(np.prod(shape[1:])) * (2 if dtype == BF16 else 4)
            self.cur = (at + nbytes + 31) & ~31
            return place(name, shape, dtype, at, regs=regs)

    zc = Z(SBUF_BASE)
    id_sb = zc.take("id_sb", [128, 128], BF16)
    e2_sb = zc.take("e2_sb", [128, 2], BF16)
    ones_sb = zc.take("ones_sb", [1, 128], BF16, regs=("ones",))
    cst = zc.take("cst", [128, 4], F32)
    s12_sb = zc.take("s12_sb", [128, 128], F32, regs=("s12",))
    s3_sb = zc.take("s3_sb", [128, DM], F32, regs=("s3",))
    bo_sb = zc.take("bo_sb", [128, DM], F32, regs=("bo",))
    SEG_A = zc.cur
    za1 = Z(SEG_A)
    attn_sb = za1.take("attn_sb", [128, 8 * seq], BF16, regs=("attn",))
    za2 = Z(SEG_A)
    x1f_sb = [za2.take(f"x1f{i}", [128, ntc * DM], F32) for i in range(2)]
    yt_sb = za2.take("yt", [128, ntc * DM], F32, regs=("yt",))
    xn2T_sb = za2.take("xn2T", [128, 8 * FT], BF16, regs=("xn2T",))
    ofb_sb = [za2.take(f"ofb{i}", [128, DM], F32) for i in range(2)]
    SEG_B = max(za1.cur, za2.cur)
    XN1REGS = tuple(f"xn1T{s}" for s in range(max(1, nsl)))
    zb1 = Z(SEG_B)
    xn1T_sb = zb1.take("xn1T", [128, 8 * seq], BF16, regs=XN1REGS)
    zb2 = Z(SEG_B)
    ab_sb = [zb2.take(f"ab{i}", [128, 32 * FT], BF16) for i in range(2)]
    wo_sb = place("wo_sb", [128, 8 * DM], BF16, SEG_B)
    SEG_C = max(zb1.cur, zb2.cur)
    z1a = Z(SEG_C)
    xa_sb = [z1a.take(f"xa{i}", [128, DM], F32) for i in range(2)]
    xnb_sb = [z1a.take(f"xnb{i}", [128, DM], BF16) for i in range(2)]
    z1b = Z(SEG_C)
    qt1_t = z1b.take("qt1", [128, seq], BF16)
    pbig_sb = [z1b.take("P0", [128, 1024], BF16, regs=("Ph0", "Ph1"))]
    q2_at = z1b.cur
    q2_sb = [z1b.take(f"q2_{i}", [128, 512], BF16) for i in range(2)]
    rhi2_sb = place("rhi2", [1, 512], BF16, q2_at, regs=("rhi2",))
    rlo2_sb = place("rlo2", [1, 512], BF16, q2_at + 1024, regs=("rlo2",))

    zl = Z(z1a.cur)
    st_sb = zl.take("st_sb", [128, 24], F32, regs=("st0", "st1"))
    mv_sb = [zl.take(f"mv{i}", [128, 2], F32) for i in range(2)]
    rs_sb = [zl.take(f"rs{i}", [128, 1], F32) for i in range(2)]
    A0_END = zl.cur
    z2a = Z(A0_END)
    assert z1b.cur <= A0_END, f"zone1 overflow: {z1b.cur} > {A0_END}"
    pbig_sb.append(z2a.take("P1", [128, 1024], BF16, regs=("P1h0", "P1h1")))
    rmx_sb = z2a.take("rmx", [2, 16], F32)
    nrm_sb = z2a.take("nrm", [2, 8], F32,
                      regs=tuple(f"nrm{i}" for i in range(6)))
    nbb_sb = z2a.take("nbb", [2, 1], BF16)
    nbr_sb = z2a.take("nbr", [1, 2], BF16, regs=("nbr", "nbr1"))
    negb_sb = [z2a.take(f"negb{i}", [128, 1], F32) for i in range(2)]
    rrow_sb = z2a.take("rrow", [1, 512], F32)
    rhi_sb = z2a.take("rhi", [1, 512], BF16)
    rlo_sb = z2a.take("rlo", [1, 512], BF16)
    rbc_sb = z2a.take("rbc", [64, 512], F32)
    qt_base = z2a.cur
    qt0_t = z2a.take("qt0", [128, seq], BF16)
    vext_sb = [z2a.take("vext0", [128, nt * 130], BF16)]
    qt_sb = [qt0_t, qt1_t]
    kt_sb = [z2a.take(f"kt{i}", [128, seq], BF16) for i in range(2)]
    wqs_sb = z2a.take("wqs", [128, 1024], BF16)
    wks_sb = z2a.take("wks", [128, 1024], BF16)
    wvs_sb = z2a.take("wvs", [128, 1024], BF16)
    assert z2a.cur <= BYTES_LIMIT, f"A1 overflow: {z2a.cur}"
    zfe = Z(qt_base)
    hs_sb = [zfe.take(f"hs{i}", [128, 4096], BF16) for i in range(2)]
    w12s_sb = [zfe.take(f"w12s{i}", [128, 1024], BF16) for i in range(4)]
    vext_sb.append(z2a.take("vext1", [128, nt * 130], BF16))
    assert z2a.cur <= BYTES_LIMIT, f"A1 overflow with vext1: {z2a.cur}"
    z2b = Z(zfe.cur)
    ot_sb = [z2b.take(f"ot{i}", [128, DM], F32) for i in range(2)]
    x1s_sb = [z2b.take(f"x1s{i}", [128, DM], F32) for i in range(2)]
    oln_sb = [z2b.take(f"oln{i}", [128, DM], F32) for i in range(2)]
    nmr_sb = [z2b.take(f"nmr{i}", [128, 1], F32) for i in range(2)]
    assert z2b.cur <= BYTES_LIMIT, f"A2 overflow: {z2b.cur}"
    zfs = Z(A0_END)
    sil_sb = [zfs.take(f"sil{i}", [128, 512], F32) for i in range(2)]
    zfl = Z(zfe.cur)
    w3q_sb = [zfl.take(f"w3q{i}", [128, 32 * 128], BF16) for i in range(2)]
    yl_sb = zfl.take("yl", [128, DM], F32)
    assert zfl.cur <= BYTES_LIMIT, f"FFN-late overflow: {zfl.cur}"

    OVERLAP = {}
    for i, (a0, e0, r0) in enumerate(RANGES):
        for (a1, e1, r1) in RANGES[i + 1:]:
            if a0 < e1 and a1 < e0:
                for r in r0:
                    OVERLAP.setdefault(r, set()).update(r1)
                for r in r1:
                    OVERLAP.setdefault(r, set()).update(r0)
    OVERLAP = {k: tuple(v - {k}) for k, v in OVERLAP.items()}

    pA = nc.alloc_psum_tensor("pA", [128, 1024], F32)
    pB = nc.alloc_psum_tensor("pB", [128, 1024], F32)
    pC = nc.alloc_psum_tensor("pC", [128, 1024], F32)
    pD = nc.alloc_psum_tensor("pD", [128, 1024], F32)
    PS = {"pA": pA, "pB": pB, "pC": pC, "pD": pD}

    def ph(name, half):
        return PS[name][:, half * 512:(half + 1) * 512], f"{name}{half}"

    def pbf(name):
        return PS[name].ap().bitcast(BF16)

    settle_sb = nc.alloc_sbuf_tensor_at("settle", [1, 16], F32, offset=229248)
    pb = PB(nc)
    pb.overlap = OVERLAP
    pb.settle_emit = lambda e: e.memset(settle_sb[0:1, 0:1], 0.0)

    pb.dma("sync", lambda e: e.dma_start(out=id_sb[:, :], in_=id_d.ap()), writes=("id_sb",))
    pb.dma("sync", lambda e: e.dma_start(out=e2_sb[:, :], in_=e2_d.ap()), writes=("e2_sb",))
    pb.dma("sync", lambda e: e.dma_start(out=s12_sb[:, :], in_=s12_d.ap()), writes=("s12",))
    pb.dma("sync", lambda e: e.dma_start(out=s3_sb[:, :], in_=s3_d.ap()), writes=("s3",))
    pb.dma("sync", lambda e: e.dma_start(out=bo_sb[:, :], in_=bo_d.ap()), writes=("bo",))
    pb.op("vector", lambda e: e.memset(ones_sb[:, :], 1.0), writes=("ones",))

    def set_cst(e):
        e.memset(cst[:, 0:1], EPS)
        e.memset(cst[:, 1:2], -0.5)
        e.memset(cst[:, 2:3], SCALE)
        return e.memset(cst[:, 3:4], 0.5)
    pb.op("vector", set_cst, writes=("cst",))

    def ln_chain(xr, xreg, b, outs):
        mv, rs = mv_sb[b], rs_sb[b]
        stc = st_sb[:, 12 * b:12 * b + 12]
        pb.op("vector", lambda e: e.bn_stats(stc[:, 0:6], xr[:, 0:512]),
              reads=(xreg,), writes=(f"st{b}",))
        pb.op("vector", lambda e: e.bn_stats(stc[:, 6:12], xr[:, 512:1024]),
              reads=(xreg, f"st{b}"), writes=(f"st{b}",))
        pb.op("vector", lambda e: e.bn_aggr(mv[:, :], stc[:, :]),
              reads=(f"st{b}",), writes=(f"mv{b}",))
        pb.op("scalar", lambda e: e.activation(rs[:, :], mv[:, 1:2], AF.Ln,
                                               bias=cst[:, 0:1], scale=1.0),
              reads=(f"mv{b}", "cst"), writes=(f"rs{b}",))
        pb.op("scalar", lambda e: e.activation(rs[:, :], rs[:, :], AF.Exp,
                                               bias=0.0, scale=cst[:, 1:2]),
              reads=(f"rs{b}", "cst"), writes=(f"rs{b}",))
        for out_ap, outreg in outs:
            pb.op("vector", lambda e, o=out_ap: e.tensor_scalar(
                o, xr[:, :], mv[:, 0:1], rs[:, 0:1], ALU.subtract, ALU.mult),
                reads=(xreg, f"mv{b}", f"rs{b}"), writes=(outreg,))

    def transpose8(src, srcreg, half, dst, dstreg):
        pv = pbf("pD")[:, 1024 * half:1024 * half + 1024]
        reg = f"pD{half}"
        for c in range(8):
            pb.op("tensor", lambda e, c=c: e.transpose(
                pv[:, c * 128:(c + 1) * 128], src[:, c * 128:(c + 1) * 128],
                id_sb[:, :]),
                reads=(srcreg, "id_sb"), writes=(reg,))
        pb.op("vector", lambda e: e.tensor_copy(
            dst, pv[:, 0:1024].rearrange("p (c s) -> p c s", c=8)),
            reads=(reg,), writes=(dstreg,))

    pb.dma("gpsimd", lambda e: e.dma_start(
        out=qkvb_d.ap(),
        in_=qkv_d.ap().rearrange("q o p c x -> q o p (c x)")),
        writes=("qkvb",))
    pb.dma("gpsimd", lambda e: e.dma_start(
        out=wob_d.ap(),
        in_=wo_d.ap().rearrange("(c p) o -> p c o", p=128)),
        writes=("wob",))

    def emit_cast_batch(hp):
        gate = ("negb0",) if hp == 0 else ("attn",)
        plans = {0: (w1_d, w1b_d, 4, 8, "w1b"), 1: (w2_d, w2b_d, 4, 8, "w2b"),
                 2: (h1_d, h1b_d, 16, 2, "h1b"), 3: (h2_d, h2b_d, 16, 2, "h2b"),
                 4: (w3_d, w3b_d, 4, 2, "w3b")}
        if hp not in plans:
            return
        src_d, dst_d, nch, per, tag = plans[hp]
        emits = []
        for i in range(nch):
            emits.append(lambda e, i=i, src_d=src_d, dst_d=dst_d, per=per:
                         e.dma_start(
                out=dst_d.ap()[per * i:per * (i + 1)],
                in_=src_d.ap().rearrange("o p c x -> o p (c x)")
                [per * i:per * (i + 1)]))
        pb.dma_chunks("gpsimd", emits, reads=gate,
                      writes=tuple(f"{tag}{o}" for o in range(nch * per)))

    tq = []

    def drain(k):
        for _ in range(min(k, len(tq))):
            tq.pop(0)()

    pdctr = [0]

    def next_pd():
        h = pdctr[0] % 2
        pdctr[0] += 1
        return PS["pD"][:, 512 * h:512 * (h + 1)], f"pD{h}"


    def make_prep(hp):
        qb = hp % 2
        qt, kt = qt_sb[qb], kt_sb[qb]
        vext = vext_sb[qb]
        vreg = f"vext{qb}"
        th = []
        for (qi, dst, dreg) in ((0, wqs_sb, "wqs"), (1, wks_sb, "wks"),
                                (2, wvs_sb, "wvs")):
            def t_load(qi=qi, dst=dst, dreg=dreg):
                pb.dma("sync", lambda e, qi=qi, dst=dst: e.dma_start(
                    out=dst[:, :], in_=qkvb_d.ap()[qi, hp]),
                    reads=("qkvb",), writes=(dreg,))
            th.append(t_load)
        th.append(lambda vext=vext, vreg=vreg:
                  pb.op("vector", lambda e, vext=vext: e.memset(vext[:, :], 1.0),
                        writes=(vreg,)))
        projs = []
        for (wsl, wreg, dst, dreg, sc) in (
                (wqs_sb, "wqs", qt, f"qt{qb}", SCALE),
                (wks_sb, "wks", kt, f"kt{qb}", None)):
            lst = []
            for s in range(nsl):
                def t_proj(wsl=wsl, wreg=wreg, dst=dst, dreg=dreg, sc=sc, s=s):
                    pp, preg = next_pd()
                    for c in range(8):
                        pb.op("tensor", lambda e, pp=pp, wsl=wsl, c=c, s=s:
                              e.matmul(pp, wsl[:, c * 128:(c + 1) * 128],
                                       xn1T_r2[:, c, s * 512:(s + 1) * 512],
                                       start=(c == 0), stop=(c == 7)),
                              reads=(wreg, f"xn1T{s}"), writes=(preg,))
                    if sc is None:
                        pb.op("vector", lambda e, pp=pp, dst=dst, s=s:
                              e.tensor_copy(dst[:, s * 512:(s + 1) * 512], pp),
                              reads=(preg,), writes=(dreg,))
                    else:
                        pb.op("vector", lambda e, pp=pp, dst=dst, s=s:
                              e.tensor_scalar(dst[:, s * 512:(s + 1) * 512],
                                              pp, SCALE, None, ALU.mult),
                              reads=(preg,), writes=(dreg,))
                lst.append(t_proj)
            projs.append(lst)
        vext_r = vext[:, :].rearrange("p (t h v) -> p t h v", t=nt, h=2)
        tvs = []
        for tc in range(nt):
            def t_v(tc=tc, vext_r=vext_r, vreg=vreg):
                pp, preg = next_pd()
                for c in range(8):
                    pb.op("tensor", lambda e, pp=pp, c=c, tc=tc: e.matmul(
                        pp[:, 0:128],
                        xn1T_r2[:, c, tc * 128:(tc + 1) * 128],
                        wvs_sb[:, c * 128:(c + 1) * 128],
                        start=(c == 0), stop=(c == 7)),
                        reads=("wvs", f"xn1T{tc // 4}"), writes=(preg,))
                pb.op("vector", lambda e, pp=pp, tc=tc, vext_r=vext_r:
                      e.tensor_copy(vext_r[:, tc, :, 0:64],
                                    pp[:, 0:128].rearrange(
                                        "p (h v) -> p h v", v=64)),
                      reads=(preg, vreg), writes=(vreg,))
            tvs.append(t_v)
        norms = []
        reduces = []
        for (srcb, sreg, col) in ((qt, f"qt{qb}", 0), (kt, f"kt{qb}", 1)):
            nlst = []
            for s in range(nsl):
                def t_norm(srcb=srcb, sreg=sreg, s=s, col=col):
                    qp = s % 2
                    pb.op("vector", lambda e, srcb=srcb, s=s, qp=qp:
                          e.tensor_mul(q2_sb[qp][:, :],
                                       srcb[:, s * 512:(s + 1) * 512],
                                       srcb[:, s * 512:(s + 1) * 512]),
                          reads=(sreg,), writes=(f"q2_{qp}",))
                    np_, npreg = next_pd()
                    pb.op("tensor", lambda e, np_=np_, qp=qp: e.matmul(
                        np_[0:2, :], e2_sb[:, :], q2_sb[qp][:, :],
                        start=True, stop=True),
                        reads=(f"q2_{qp}", "e2_sb"), writes=(npreg,))
                    pb.op("vector", lambda e, np_=np_, s=s, col=col:
                          e.reduce_max(rmx_sb[:, col * 8 + s:col * 8 + s + 1],
                                       np_[0:2, :], axis=AX.X),
                          reads=(npreg, "rmx"), writes=("rmx",))
                nlst.append(t_norm)
            norms.append(nlst)
            reduces.append(lambda col=col:
                           pb.op("vector", lambda e, col=col: e.reduce_max(
                               nrm_sb[:, col:col + 1],
                               rmx_sb[:, col * 8:col * 8 + nsl], axis=AX.X),
                               reads=("rmx",), writes=(f"nrm{col}",)))
        for s in range(nsl):
            th.append(projs[0][s])
            th.append(projs[1][s])
            th.extend(tvs[4 * s:4 * s + 4])
        th.extend(norms[0])
        th.extend(norms[1])
        th.extend(reduces)
        th.append(lambda: pb.op("vector", lambda e: e.tensor_mul(
            nrm_sb[:, 2:3], nrm_sb[:, 0:1], nrm_sb[:, 1:2]),
            reads=("nrm0", "nrm1"), writes=("nrm2",)))
        th.append(lambda: pb.op("scalar", lambda e: e.activation(
            nrm_sb[:, 3:4], nrm_sb[:, 2:3], AF.Ln),
            reads=("nrm2",), writes=("nrm3",)))
        th.append(lambda: pb.op("scalar", lambda e: e.activation(
            nrm_sb[:, 4:5], nrm_sb[:, 3:4], AF.Exp, scale=cst[0:2, 3:4]),
            reads=("nrm3", "cst"), writes=("nrm4",)))
        th.append(lambda: pb.op("vector", lambda e: e.tensor_scalar(
            nrm_sb[:, 5:6], nrm_sb[:, 4:5], -1.0, None, ALU.mult),
            reads=("nrm4",), writes=("nrm5",)))
        th.append(lambda: pb.op("vector", lambda e: e.tensor_copy(
            nbb_sb[:, :], nrm_sb[:, 5:6]),
            reads=("nrm5",), writes=("nbb",)))

        def t_negb(qb=qb):
            hh = pdctr[0] % 2
            pdctr[0] += 1
            pv = pbf("pD")
            pb.op("tensor", lambda e, pv=pv, hh=hh: e.transpose(
                pv[0:1, 1024 * hh:1024 * hh + 2], nbb_sb[:, :], id_sb[0:2, 0:2]),
                reads=("nbb", "id_sb"), writes=(f"pD{hh}",))
            pb.op("vector", lambda e, pv=pv, hh=hh: e.tensor_copy(
                nbr_sb[:, :], pv[0:1, 1024 * hh:1024 * hh + 2]),
                reads=(f"pD{hh}",), writes=("nbr",))
            pb.op("vector", lambda e: e.tensor_reduce(
                nbr_sb[:, 0:1], nbr_sb[:, :], AX.X, ALU.min),
                reads=("nbr",), writes=("nbr1",))
            np_, npreg = next_pd()
            pb.op("tensor", lambda e, np_=np_: e.matmul(
                np_[:, 0:1], ones_sb[0:1, :], nbr_sb[:, 0:1],
                start=True, stop=True),
                reads=("nbr1", "ones"), writes=(npreg,))
            pb.op("vector", lambda e, np_=np_, qb=qb: e.tensor_copy(
                negb_sb[qb][:, :], np_[:, 0:1]),
                reads=(npreg,), writes=(f"negb{qb}",))
        th.append(t_negb)
        return th

    def make_wo_thunks():
        wob_f = wob_d.ap().rearrange("p c o -> p (c o)")
        th = []
        for i in range(4):
            def t_wo(i=i):
                pb.dma("sync", lambda e, i=i: e.dma_start(
                    out=wo_sb[:, 2048 * i:2048 * (i + 1)],
                    in_=wob_f[:, 2048 * i:2048 * (i + 1)]),
                    reads=("wob",), writes=("wo_sb",))
            th.append(t_wo)
        return th

    xn1T_r = xn1T_sb[:, :].rearrange("p (c s) -> p c s", c=8)
    xn1T_r2 = xn1T_r
    th0 = make_prep(0)
    for t_ in th0[:4]:
        t_()
    for t in range(nt):
        b = t % 2
        xa = xa_sb[b]
        pb.dma("sync", lambda e, t=t, xa=xa: e.dma_start(
            out=xa[:, :], in_=x_d.ap()[128 * t:128 * (t + 1), :]),
            writes=(f"xa{b}",))
        ln_chain(xa[:, :], f"xa{b}", b, [(xnb_sb[b][:, :], f"xnb{b}")])
        transpose8(xnb_sb[b], f"xnb{b}", b,
                   xn1T_r[:, :, t * 128:(t + 1) * 128], f"xn1T{t // 4}")
        if t % 4 == 3:
            for t_ in th0[4 + 6 * (t // 4):4 + 6 * (t // 4 + 1)]:
                t_()
    for t_ in th0[4 + 6 * nsl:]:
        t_()


    for hp in range(NPAIR):
        qb = hp % 2
        qt, kt = qt_sb[qb], kt_sb[qb]
        vext_l = vext_sb[qb][:, :].rearrange("p (t y) -> p t y", y=130)
        negb = negb_sb[qb]
        emit_cast_batch(hp)
        tq.extend(make_prep(hp + 1) if hp + 1 < NPAIR else make_wo_thunks())
        rate = max(1, -(-len(tq) // max(1, nqt * nt - 8)))
        for q in range(nqt):
            PKEYS = (("Ph0", "Ph1"), ("P1h0", "P1h1"))

            def emit_pv(c):
                sb = c % 2
                pbt = pbig_sb[sb]
                for a in range(2):
                    pvx, pvreg = ph("pC", a)
                    pb.op("tensor", lambda e, pvx=pvx, c=c, a=a, pbt=pbt,
                          vext_l=vext_l: e.matmul(
                        pvx[0:65, :], vext_l[:, c, 65 * a:65 * a + 65],
                        pbt[:, 512 * a:512 * (a + 1)],
                        start=(c == 0), stop=(c == nt - 1)),
                        reads=(PKEYS[sb][a], f"vext{qb}"), writes=(pvreg,))
            for c in range(nt):
                sb = c % 2
                pn = "pA" if sb == 0 else "pB"
                sa, sareg = ph(pn, 0)
                sbb, sbreg = ph(pn, 1)
                pb.op("tensor", lambda e, sa=sa, c=c, q=q, kt=kt, qt=qt: e.matmul(
                    sa, kt[0:64, c * 128:(c + 1) * 128],
                    qt[0:64, q * QTW:(q + 1) * QTW], start=True, stop=True),
                    reads=(f"kt{qb}", f"qt{qb}"), writes=(sareg,))
                pb.op("tensor", lambda e, sbb=sbb, c=c, q=q, kt=kt, qt=qt: e.matmul(
                    sbb, kt[64:128, c * 128:(c + 1) * 128],
                    qt[64:128, q * QTW:(q + 1) * QTW], start=True, stop=True),
                    reads=(f"kt{qb}", f"qt{qb}"), writes=(sbreg,))
                pb.op("scalar", lambda e, pn=pn, sb=sb, negb=negb: e.activation(
                    pbig_sb[sb][:, :], PS[pn][:, 0:1024], AF.Exp,
                    bias=negb[:, 0:1]),
                    reads=(sareg, sbreg, f"negb{qb}"),
                    writes=PKEYS[sb])
                if c > 0:
                    emit_pv(c - 1)
                if c < nt - 3:
                    drain(rate)
            emit_pv(nt - 1)
            sets = (((rrow_sb, rhi_sb, rlo_sb), ("rrow", "rhi", "rlo")),
                    ((rrow_sb, rhi2_sb, rlo2_sb), ("rrow", "rhi2", "rlo2")))
            for a, (bufs, regs) in enumerate(sets):
                pvx, pvreg = ph("pC", a)
                ro, hi, lo = bufs
                r0, r1, r2 = regs
                pb.op("vector", lambda e, pvx=pvx, ro=ro: e.reciprocal(
                    ro[:, :], pvx[64:65, :]),
                    reads=(pvreg,), writes=(r0,))
                pb.op("vector", lambda e, ro=ro, hi=hi: e.tensor_copy(
                    hi[:, :], ro[:, :]),
                    reads=(r0,), writes=(r1,))
                pb.op("vector", lambda e, ro=ro, hi=hi, lo=lo: e.tensor_sub(
                    lo[:, :], ro[:, :], hi[:, :]),
                    reads=(r0, r1), writes=(r2,))
            bcs = []
            for a, (bufs, regs) in enumerate(sets):
                ro, hi, lo = bufs
                r0, r1, r2 = regs
                bc, bcreg = next_pd()
                bcs.append((bc, bcreg))
                pb.op("tensor", lambda e, bc=bc, hi=hi: e.matmul(
                    bc[0:64, :], ones_sb[0:1, 0:64], hi[:, :],
                    start=True, stop=False),
                    reads=(r1, "ones"), writes=(bcreg,))
                pb.op("tensor", lambda e, bc=bc, lo=lo: e.matmul(
                    bc[0:64, :], ones_sb[0:1, 0:64], lo[:, :],
                    start=False, stop=True),
                    reads=(r2, "ones"), writes=(bcreg,))
            for a in range(2):
                pvx, pvreg = ph("pC", a)
                bc, bcreg = bcs[a]
                pb.op("vector", lambda e, bc=bc: e.tensor_copy(
                    rbc_sb[:, :], bc[0:64, :]),
                    reads=(bcreg, "rbc"), writes=("rbc",))
                pb.op("vector", lambda e, pvx=pvx, hp=hp, a=a, q=q: e.tensor_mul(
                    attn_sb[64 * a:64 * a + 64,
                            hp * seq + q * QTW:hp * seq + (q + 1) * QTW],
                    pvx[0:64, :], rbc_sb[:, :]),
                    reads=(pvreg, "rbc"), writes=("attn",))
            drain(3 * rate)
        drain(len(tq))

    def load_fslab(dram, dst, dstreg, idx, c=8, srcreg=None):
        pb.dma("sync", lambda e: e.dma_start(out=dst[:, :], in_=dram.ap()[idx]),
               reads=(srcreg,) if srcreg else (), writes=(dstreg,))

    def preload_s1():
        load_fslab(w1b_d, w12s_sb[0], "w12s0", 0, srcreg="w1b0")
        load_fslab(w2b_d, w12s_sb[2], "w12s2", 0, srcreg="w2b0")
        load_fslab(w1b_d, w12s_sb[1], "w12s1", 1, srcreg="w1b1")
        load_fslab(w2b_d, w12s_sb[3], "w12s3", 1, srcreg="w2b1")

    attn_r = attn_sb[:, :].rearrange("p (c s) -> p c s", c=8)
    wo_r = wo_sb[:, :].rearrange("p (c o) -> p c o", c=8)
    for t in range(nt):
        b = t % 2
        pn = "pA" if b == 0 else "pB"
        xs = xa_sb[b]
        pb.dma("sync", lambda e, t=t, xs=xs: e.dma_start(
            out=xs[:, :], in_=x_d.ap()[128 * t:128 * (t + 1), :]),
            writes=(f"xa{b}",))
        for h in range(2):
            po, poreg = ph(pn, h)
            for c in range(8):
                pb.op("tensor", lambda e, po=po, c=c, h=h, t=t: e.matmul(
                    po, attn_r[:, c, t * 128:(t + 1) * 128],
                    wo_r[:, c, h * 512:(h + 1) * 512],
                    start=(c == 0), stop=(c == 7)),
                    reads=("attn", "wo_sb"), writes=(poreg,))
        pb.op("vector", lambda e, pn=pn, b=b: e.tensor_add(
            ot_sb[b][:, :], PS[pn][:, :], bo_sb[:, :]),
            reads=(pn + "0", pn + "1", "bo"), writes=(f"ot{b}",))
        ot = ot_sb[b]
        stc = st_sb[:, 12 * b:12 * b + 12]
        mv, rs = mv_sb[b], rs_sb[b]
        pb.op("vector", lambda e, ot=ot, stc=stc: e.bn_stats(stc[:, 0:6], ot[:, 0:512]),
              reads=(f"ot{b}",), writes=(f"st{b}",))
        pb.op("vector", lambda e, ot=ot, stc=stc: e.bn_stats(stc[:, 6:12], ot[:, 512:1024]),
              reads=(f"ot{b}", f"st{b}"), writes=(f"st{b}",))
        pb.op("vector", lambda e, mv=mv, stc=stc: e.bn_aggr(mv[:, :], stc[:, :]),
              reads=(f"st{b}",), writes=(f"mv{b}",))
        pb.op("scalar", lambda e, mv=mv, rs=rs: e.activation(
            rs[:, :], mv[:, 1:2], AF.Ln, bias=cst[:, 0:1], scale=1.0),
            reads=(f"mv{b}", "cst"), writes=(f"rs{b}",))
        pb.op("scalar", lambda e, rs=rs: e.activation(
            rs[:, :], rs[:, :], AF.Exp, bias=0.0, scale=cst[:, 1:2]),
            reads=(f"rs{b}", "cst"), writes=(f"rs{b}",))
        pb.op("vector", lambda e, mv=mv, rs=rs, b=b: e.tensor_scalar(
            nmr_sb[b][:, :], mv[:, 0:1], rs[:, 0:1], -1.0, ALU.mult, ALU.mult),
            reads=(f"mv{b}", f"rs{b}"), writes=(f"nmr{b}",))
        pb.op("scalar", lambda e, ot=ot, rs=rs, b=b: e.activation(
            oln_sb[b][:, :], ot[:, :], AF.Identity, bias=nmr_sb[b][:, 0:1],
            scale=rs[:, 0:1]),
            reads=(f"ot{b}", f"rs{b}", f"nmr{b}"), writes=(f"oln{b}",))
        pb.op("vector", lambda e, b=b: e.tensor_add(
            x1s_sb[b][:, :], oln_sb[b][:, :], xa_sb[b][:, :]),
            reads=(f"oln{b}", f"xa{b}"), writes=(f"x1s{b}",))
        pb.dma("sync", lambda e, t=t, b=b: e.dma_start(
            out=x1_d.ap()[128 * t:128 * (t + 1), :], in_=x1s_sb[b][:, :]),
            reads=(f"x1s{b}",), writes=(f"x1d{t // ntc}",))
        if t == max(3, nt - 4):
            preload_s1()
            load_fslab(h1b_d, hs_sb[0], "hs0", 0, srcreg="h1b0")
            load_fslab(h1b_d, hs_sb[1], "hs1", 1, srcreg="h1b1")

    s1c, s2c, sh1c, sh2c = (s12_sb[:, 0:32], s12_sb[:, 32:64],
                            s12_sb[:, 64:96], s12_sb[:, 96:128])
    xn2T_r = xn2T_sb[:, :].rearrange("p (c s) -> p c s", c=8)

    def emit_prologue(ft):
        fb = ft % 2
        x1f = x1f_sb[fb]
        pb.dma("sync", lambda e, ft=ft, x1f=x1f: e.dma_start(
            out=x1f[:, :].rearrange("p (n d) -> p n d", n=ntc),
            in_=x1_d.ap()[ft * FT:(ft + 1) * FT, :].rearrange(
                "(n p) d -> p n d", p=128)),
            reads=(f"x1d{ft}",), writes=(f"x1f{fb}",))
        for tc in range(ntc):
            b = tc % 2
            ln_chain(x1f[:, tc * DM:(tc + 1) * DM], f"x1f{fb}", b,
                     [(xnb_sb[b][:, :], f"xnb{b}")])
            transpose8(xnb_sb[b], f"xnb{b}", b,
                       xn2T_r[:, :, tc * 128:(tc + 1) * 128], "xn2T")

    emit_prologue(0)
    for ft in range(nft):
        fb = ft % 2
        x1f = x1f_sb[fb]
        if ft == 0:
            load_fslab(w3b_d, w3q_sb[0], "w3q0", 0, srcreg="w3b0")
            load_fslab(w3b_d, w3q_sb[1], "w3q1", 1, srcreg="w3b1")
        for o in range(32):
            b = o % 2
            pn = "pA" if b == 0 else "pB"
            u1, u1reg = ph(pn, 0)
            u2, u2reg = ph(pn, 1)
            w1s, w2s = w12s_sb[b], w12s_sb[2 + b]
            for c in range(8):
                pb.op("tensor", lambda e, u1=u1, w1s=w1s, c=c: e.matmul(
                    u1, w1s[:, c * 128:(c + 1) * 128], xn2T_r[:, c, :],
                    start=(c == 0), stop=(c == 7)),
                    reads=(f"w12s{b}", "xn2T"), writes=(u1reg,))
            for c in range(8):
                pb.op("tensor", lambda e, u2=u2, w2s=w2s, c=c: e.matmul(
                    u2, w2s[:, c * 128:(c + 1) * 128], xn2T_r[:, c, :],
                    start=(c == 0), stop=(c == 7)),
                    reads=(f"w12s{2 + b}", "xn2T"), writes=(u2reg,))
            pb.op("scalar", lambda e, u1=u1, o=o, b=b: e.activation(
                sil_sb[b][:, :], u1, AF.Silu, scale=s1c[:, o:o + 1]),
                reads=(u1reg, "s12"), writes=(f"sil{b}",))
            pb.op("vector", lambda e, u2=u2, o=o, b=b: e.scalar_tensor_tensor(
                ab_sb[0][:, o * 512:(o + 1) * 512], u2, s2c[:, o:o + 1],
                sil_sb[b][:, :], ALU.mult, ALU.mult),
                reads=(u2reg, "s12", f"sil{b}"), writes=("ab0",))
            if o + 2 < 32:
                load_fslab(w1b_d, w12s_sb[b], f"w12s{b}", o + 2, srcreg=f"w1b{o + 2}")
                load_fslab(w2b_d, w12s_sb[2 + b], f"w12s{2 + b}", o + 2, srcreg=f"w2b{o + 2}")
        if ft + 1 < nft:
            emit_prologue(ft + 1)
            preload_s1()
        for (hd_, hname, src, sreg, dst, dreg, sc, nxt) in (
                (h1b_d, "h1b", ab_sb[0], "ab0", ab_sb[1], "ab1", sh1c,
                 (h2b_d, "h2b")),
                (h2b_d, "h2b", ab_sb[1], "ab1", ab_sb[0], "ab0", sh2c,
                 (h1b_d, "h1b") if ft + 1 < nft else None)):
            src_r = src[:, :].rearrange("p (c s) -> p c s", c=32)
            for o in range(32):
                b = o % 2
                pn = "pA" if b == 0 else "pB"
                uu, ureg = ph(pn, 0)
                hsl = hs_sb[b]
                for c in range(32):
                    pb.op("tensor", lambda e, uu=uu, hsl=hsl, c=c, src_r=src_r:
                          e.matmul(uu, hsl[:, c * 128:(c + 1) * 128], src_r[:, c, :],
                                   start=(c == 0), stop=(c == 31)),
                          reads=(f"hs{b}", sreg), writes=(ureg,))
                pb.op("scalar", lambda e, uu=uu, o=o, dst=dst, sc=sc: e.activation(
                    dst[:, o * 512:(o + 1) * 512], uu, AF.Gelu,
                    scale=sc[:, o:o + 1]),
                    reads=(ureg, "s12"), writes=(dreg,))
                if o + 2 < 32:
                    load_fslab(hd_, hs_sb[b], f"hs{b}", o + 2, srcreg=f"{hname}{o + 2}")
                elif nxt is not None:
                    load_fslab(nxt[0], hs_sb[b], f"hs{b}", o - 30,
                               srcreg=f"{nxt[1]}{o - 30}")
        cT_r = ab_sb[0][:, :].rearrange("p (c s) -> p c s", c=32)
        for qd in range(8):
            b = qd % 2
            w3s = w3q_sb[b]
            for tc in range(ntc):
                yy, yreg = ph("pC", tc % 2)
                for c in range(32):
                    pb.op("tensor", lambda e, yy=yy, w3s=w3s, c=c, tc=tc: e.matmul(
                        yy[:, 0:128], cT_r[:, c, tc * 128:(tc + 1) * 128],
                        w3s[:, c * 128:(c + 1) * 128],
                        start=(c == 0), stop=(c == 31)),
                        reads=(f"w3q{b}", "ab0"), writes=(yreg,))
                pb.op("vector", lambda e, yy=yy, tc=tc, qd=qd: e.tensor_mul(
                    yt_sb[:, tc * DM + qd * 128:tc * DM + (qd + 1) * 128],
                    yy[:, 0:128], s3_sb[:, qd * 128:(qd + 1) * 128]),
                    reads=(yreg, "s3"), writes=("yt",))
            if qd + 2 < 8:
                load_fslab(w3b_d, w3q_sb[b], f"w3q{b}", qd + 2, srcreg=f"w3b{qd + 2}")
            elif ft + 1 < nft:
                load_fslab(w3b_d, w3q_sb[b], f"w3q{b}", qd - 6, srcreg=f"w3b{qd - 6}")
        for tc in range(ntc):
            b = tc % 2
            ln_chain(yt_sb[:, tc * DM:(tc + 1) * DM], "yt", b,
                     [(yl_sb[:, :], "yl")])
            pb.op("vector", lambda e, b=b, tc=tc, x1f=x1f: e.tensor_add(
                ofb_sb[b][:, :], yl_sb[:, :], x1f[:, tc * DM:(tc + 1) * DM]),
                reads=("yl", f"x1f{fb}"), writes=(f"ofb{b}",))
            pb.dma("sync", lambda e, ft=ft, tc=tc, b=b: e.dma_start(
                out=out_d.ap()[ft * FT + tc * 128:ft * FT + (tc + 1) * 128, :],
                in_=ofb_sb[b][:, :]),
                reads=(f"ofb{b}",), writes=("outd",))

    pb.barrier()
    pb.build()
    pb.emit()
    nc.finalize()
    return nc


def host_prep(inputs, seq=4096):
    bf = ml_dtypes.bfloat16
    f32 = np.float32

    def cc(a):
        return np.ascontiguousarray(a, dtype=f32)

    Wq = np.asarray(inputs["L0_attn_Wq"], dtype=f32)
    Wk = np.asarray(inputs["L0_attn_Wk"], dtype=f32)
    Wv = np.asarray(inputs["L0_attn_Wv"], dtype=f32)
    Wo = np.asarray(inputs["L0_attn_Wo"], dtype=f32)
    bo = np.asarray(inputs["L0_attn_bo"], dtype=f32)
    w1 = np.asarray(inputs["L0_ffn_w1_w"], dtype=f32)
    w2 = np.asarray(inputs["L0_ffn_w2_w"], dtype=f32)
    h1 = np.asarray(inputs["L0_ffn_h1_w"], dtype=f32)
    h2 = np.asarray(inputs["L0_ffn_h2_w"], dtype=f32)
    w3 = np.asarray(inputs["L0_ffn_w3_w"], dtype=f32)
    s1 = np.asarray(inputs["L0_ffn_w1_s"], dtype=f32)
    s2 = np.asarray(inputs["L0_ffn_w2_s"], dtype=f32)
    sh1 = np.asarray(inputs["L0_ffn_h1_s"], dtype=f32)
    sh2 = np.asarray(inputs["L0_ffn_h2_s"], dtype=f32)
    s3 = np.asarray(inputs["L0_ffn_w3_s"], dtype=f32)

    def pack_qkv(W):
        return cc(W.T.reshape(8, 128, 8, 128).transpose(2, 1, 0, 3))

    def pack_ffn(W, oc, och):
        din = W.shape[1]
        return cc(W.T.reshape(din // 128, 128, oc, och).transpose(2, 1, 0, 3))

    s12 = np.zeros((128, 128), dtype=f32)
    s12[:, 0:32] = s1.reshape(32, 128).T
    s12[:, 32:64] = s2.reshape(32, 128).T
    s12[:, 64:96] = sh1.reshape(32, 128).T
    s12[:, 96:128] = sh2.reshape(32, 128).T

    e2 = np.zeros((128, 2), dtype=bf)
    e2[0:64, 0] = 1
    e2[64:128, 1] = 1

    shared = {
        "qkv": np.stack([pack_qkv(Wq), pack_qkv(Wk), pack_qkv(Wv)], axis=0),
        "wo": cc(Wo.T),
        "bo": cc(np.broadcast_to(bo, (128, DM))),
        "w1": pack_ffn(w1, 32, 128), "w2": pack_ffn(w2, 32, 128),
        "h1": pack_ffn(h1, 32, 128), "h2": pack_ffn(h2, 32, 128),
        "w3": pack_ffn(w3, 8, 128),
        "s12": s12,
        "s3": cc(np.broadcast_to(s3, (128, DM))),
        "idm": np.eye(128, dtype=bf),
        "e2": e2,
    }
    x = np.asarray(inputs["L0_attn_x"], dtype=f32)
    in_maps = []
    for b in range(x.shape[0]):
        m = dict(shared)
        m["x"] = cc(x[b][:seq])
        in_maps.append(m)
    return in_maps


def kernel(**inputs):
    seq = 4096
    in_maps = host_prep(inputs, seq)
    nc = build(seq)
    res = run_bass_kernel_spmd(nc, in_maps, core_ids=list(range(8)))
    return np.stack([r["out"] for r in res.results], axis=0)
